# version 23
# baseline (speedup 1.0000x reference)
import numpy as np
import ml_dtypes
import concourse.bass as bass
import concourse.mybir as mybir
from concourse.bass_utils import run_bass_kernel_spmd

F32 = mybir.dt.float32
BF16 = mybir.dt.bfloat16
U8 = mybir.dt.uint8
AF = mybir.ActivationFunctionType
ALU = mybir.AluOpType
AX = mybir.AxisListType

S, D, NH, HD = 4096, 1024, 8, 64
INNER, HID, NE = 512, 2048, 8
T, NT = 512, 8
QT, NQ, NST = 1024, 4, 2
NCORES = 8

SEMKEYS = ("tensor", "vector", "scalar", "gpsimd", "sync", "dsync", "dgp")


ENGS = ("tensor", "vector", "scalar", "gpsimd", "sync")


class Prog:

    def __init__(self, nc):
        self.nc = nc
        self.ops = []
        self.tags = {}
        self.cnt = {k: 0 for k in ENGS}
        self.sems = {k: nc.alloc_semaphore("sem_" + k) for k in ENGS}
        self.fam_last = {}
        self.barriers = []
        self._auto = 0

    def op(self, eng, fn, waits=(), tag=None, dma=False, fam=None, force=False):
        waits = list(waits)
        key = eng
        if dma:
            assert fam is not None, "dma op needs a family"
            key = "f_" + fam
            if key not in self.sems:
                self.sems[key] = self.nc.alloc_semaphore("sem_" + key)
                self.cnt[key] = 0
            if tag is None:
                tag = f"_auto{self._auto}"
                self._auto += 1
            if fam in self.fam_last:
                waits.append(self.fam_last[fam])
            self.fam_last[fam] = tag
        for w in waits:
            assert w in self.tags, f"wait on undefined tag {w}"
        if tag is not None:
            assert tag not in self.tags, f"duplicate tag {tag}"
            self.cnt[key] += 16 if dma else 1
            self.tags[tag] = (key, self.cnt[key])
        self.ops.append((eng, fn, tuple(waits), tag, key if tag is not None else None, force))

    def barrier(self):
        self.barriers.append(len(self.ops))


def _seg_emitter(nc, prog, seg, lastw, block):
    for engname in ENGS:
        ops_e = [o for o in seg if o[0] == engname]
        if not ops_e:
            continue

        def body(eng, ops_e=ops_e, engname=engname):
            lw = lastw[engname]
            for (_, fn, waits, tag, inc_key, force) in ops_e:
                for w in waits:
                    key, val = prog.tags[w]
                    if key == engname and engname != "gpsimd" and not force:
                        continue
                    if key != engname and not force and lw.get(key, 0) >= val:
                        continue
                    eng.wait_ge(prog.sems[key], val)
                    if key != engname:
                        lw[key] = max(lw.get(key, 0), val)
                instr = fn(eng) if fn is not None else None
                if tag is not None:
                    assert instr is not None
                    instr.then_inc(prog.sems[inc_key], 16 if inc_key.startswith("f_") else 1)
                    if not inc_key.startswith("f_"):
                        lw[inc_key] = max(lw.get(inc_key, 0), prog.tags[tag][1])

        getattr(block, engname)(body)


def emit_all(nc, prog):
    bounds = prog.barriers + [len(prog.ops)]
    cnt = {k: 0 for k in ENGS}
    marks = []
    bi = 0
    for idx, (eng, fn, waits, tag, inc_key, force) in enumerate(prog.ops):
        while bi < len(prog.barriers) and idx == prog.barriers[bi]:
            marks.append(dict(cnt))
            bi += 1
        if tag is not None and not inc_key.startswith("f_"):
            cnt[inc_key] += 1
    while bi < len(prog.barriers):
        marks.append(dict(cnt))
        bi += 1

    lastw = {e: {} for e in ENGS}
    start = 0
    for seg_i, end in enumerate(bounds):
        seg = prog.ops[start:end]
        if seg_i > 0:
            m = marks[seg_i - 1]
            for e in lastw:
                for k in ENGS:
                    lastw[e][k] = max(lastw[e].get(k, 0), m[k])
        if seg:
            with nc.Block() as block:
                _seg_emitter(nc, prog, seg, lastw, block)
        start = end


class Mem:
    def __init__(self, nc, base, cap):
        self.nc = nc
        self.base = base
        self.cap = cap
        self.off = 0

    def alloc(self, name, shape, dtype):
        nbytes = int(np.prod(shape[1:])) * (4 if dtype == F32 else 2)
        nbytes = (nbytes + 255) // 256 * 256
        h = self.nc.alloc_sbuf_tensor_at(name, list(shape), dtype,
                                         offset=self.base + self.off)
        self.off += nbytes
        assert self.off <= self.cap, f"SBUF overflow at {name}: {self.off} > {self.cap}"
        return h

    def mark(self):
        return self.off

    def reset(self, off):
        self.off = off


def build_nc(debug=(), stop_after="full", n_tiles=NT, a_stage="all", b_experts=NE, b_norm=True):
    nc = bass.Bass("TRN2", target_bir_lowering=False, detect_race_conditions=False)

    dI = lambda n, s: nc.dram_tensor(n, s, F32, kind="ExternalInput")
    x_hbm = dI("x", [S, D])
    Wq_h, Wk_h, Wv_h = dI("Wq", [D, INNER]), dI("Wk", [D, INNER]), dI("Wv", [D, INNER])
    Wo_h = dI("Wo", [INNER, D])
    bq_h, bk_h, bv_h = dI("bq", [INNER]), dI("bk", [INNER]), dI("bv", [INNER])
    bo_h = dI("bo", [D])
    alpha_h = dI("alpha", [NH * HD])
    augp_h = dI("augp", [NH * HD])
    lns_h, lnb_h = dI("lns", [D]), dI("lnb", [D])
    rw_h, rb_h = dI("rw", [D, NE]), dI("rb", [NE])
    sw1_h, sb1_h = dI("sw1", [D, HID]), dI("sb1", [HID])
    sw2_h, sb2_h = dI("sw2", [HID, D]), dI("sb2", [D])
    rw1_h, rb1_h = dI("rw1", [NE, D, HID]), dI("rb1", [NE, HID])
    rw2_h, rb2_h = dI("rw2", [NE, HID, D]), dI("rb2", [NE, D])
    id_h = dI("id128", [128, 128])
    hmat_h = nc.dram_tensor("hmat", [128, 4, NH], BF16, kind="ExternalInput")
    mconst_h = dI("mconst", [NH, INNER])
    out_hbm = nc.dram_tensor("out", [S, D], F32, kind="ExternalOutput")
    h32_hbm = nc.dram_tensor("h32scr", [8, 128, S], F32, kind="Internal")
    h16_hbm = nc.dram_tensor("h16scr", [8, 128, S], BF16, kind="Internal")
    dbg_t = {}
    for name, shape, dt in debug:
        dbg_t[name] = nc.dram_tensor("dbg_" + name, shape, dt, kind="ExternalOutput")

    remaining = nc.sbuf_bytes_remaining
    arena_bytes = remaining - 1024
    arena = nc.alloc_sbuf_tensor("arena", [128, arena_bytes], U8)
    base = nc.lookup_mloc(arena).addr
    mem = Mem(nc, base, arena_bytes)

    id_sb = mem.alloc("id_sb", [128, 128], F32)
    hmat = mem.alloc("hmat_sb", [128, 4, NH], BF16)
    onescol_f = mem.alloc("onescol_f", [128, 1], F32)
    onescol_b = mem.alloc("onescol_b", [128, 1], BF16)
    ones1_f = mem.alloc("ones1_f", [1, 128], F32)
    ones1_b = mem.alloc("ones1_b", [1, 128], BF16)
    ones8c = mem.alloc("ones8c", [8, 1], F32)
    w_hm = mem.alloc("w_hm", [32, S], BF16)
    rb2_sb = mem.alloc("rb2_sb", [8, D], BF16)
    cols = mem.alloc("cols", [128, 204], F32)
    C_DEC, C_AUG, C_BQ, C_BK, C_BV, C_BO = 0, 4, 8, 12, 16, 20
    C_LNS, C_LNB, C_SB2, C_SB1, C_RB1 = 28, 36, 44, 52, 68
    rb8_off = mem.mark()
    rb8col = mem.alloc("rb8col", [8, 1], F32)
    bosum = nc.alloc_sbuf_tensor_at("bosum", [1, 8], F32, offset=base + rb8_off + 32)
    wosum = nc.alloc_sbuf_tensor_at("wosum", [128, 4], BF16, offset=base + rb8_off + 64)
    rwsb = mem.alloc("rwsb", [128, 8, NE], BF16)
    union = mem.mark()

    x_tm = mem.alloc("x_tm", [128, 4, D], F32)
    rowstage = nc.alloc_sbuf_tensor_at("rowstage", [1, 2048], F32,
                                       offset=base + union)
    x_fm = [mem.alloc("x_fm0", [128, 8, T], F32), mem.alloc("x_fm1", [128, 8, T], F32)]
    x_bf = mem.alloc("x_bf", [128, 8, T], BF16)
    wq = mem.alloc("wq", [128, 8, INNER], BF16)
    wk = mem.alloc("wk", [128, 8, INNER], BF16)
    wv = mem.alloc("wv", [128, 8, INNER], BF16)
    wo = mem.alloc("wo", [128, 4, D], BF16)
    q_sb = [mem.alloc("q_sb0", [128, 4, T], BF16), mem.alloc("q_sb1", [128, 4, T], BF16)]
    k_sb = [mem.alloc("k_sb0", [128, 4, T], BF16), mem.alloc("k_sb1", [128, 4, T], BF16)]
    v_sb = [mem.alloc("v_sb0", [128, 4, T], BF16), mem.alloc("v_sb1", [128, 4, T], BF16)]
    sqk = mem.alloc("sqk", [128, 4, T], BF16)
    hs32 = mem.alloc("hs32", [32, T], F32)
    sm1 = mem.alloc("sm1", [32, T], F32)
    sm2 = mem.alloc("sm2", [32, T], F32)
    sm3 = mem.alloc("sm3", [32, T], F32)
    row_nm = mem.alloc("row_nm", [1, T], F32)
    row_ms = mem.alloc("row_ms", [1, T], F32)
    row_t = mem.alloc("row_t", [1, T], F32)
    row_rb = mem.alloc("row_rb", [1, T], BF16)
    row_mb = mem.alloc("row_mb", [1, T], BF16)
    arow_t = mem.alloc("arow_t", [1, T], F32)
    arow_rb = mem.alloc("arow_rb", [1, T], BF16)
    u_y = mem.alloc("u_y0", [128, 4, T], BF16)
    aug_bf = mem.alloc("aug_bf", [128, 1, T], BF16)
    itmp = nc.alloc_sbuf_tensor_at("itmp", [128, T], BF16,
                                   offset=nc.lookup_mloc(sm1).addr)
    inner_bf = mem.alloc("inner_bf", [128, 4, T], BF16)
    bufo = mem.alloc("bufo", [128, 8, T], F32)
    c1 = mem.alloc("c1", [128, 8, T], F32)
    o2_bf = mem.alloc("o2_bf", [128, 8, T], BF16)
    h_bf = mem.alloc("h_bf", [128, 8, T], BF16)
    B_sb = mem.alloc("B_sb", [8, INNER], BF16)
    attn_bf = nc.alloc_sbuf_tensor_at("attn_bf", [8, T], BF16,
                                      offset=nc.lookup_mloc(sm3).addr)
    a_end = mem.mark()

    mem.reset(union)
    acc = mem.alloc("acc", [128, 8, QT], F32)
    hbB = mem.alloc("hbB", [128, 8, QT], BF16)
    W1 = mem.alloc("W1", [128, 8, HID], BF16)
    W2 = mem.alloc("W2", [128, 16, D], BF16)
    h1a = mem.alloc("h1a", [128, 16, T], BF16)
    h1b = mem.alloc("h1b", [128, 16, T], BF16)
    h1 = [h1a, h1b]
    wbc = mem.alloc("wbc", [128, T], BF16)
    wtmp = [mem.alloc("wtmp0", [32, T], BF16), mem.alloc("wtmp1", [32, T], BF16)]
    out_tm = nc.alloc_sbuf_tensor_at("out_tm", [128, 4, D], F32,
                                     offset=nc.lookup_mloc(h1b).addr)
    WP = mem.alloc("WP", [128, 8, HID], BF16)
    sq_bf = nc.alloc_sbuf_tensor_at("sq_bf", [128, 8, T], BF16,
                                    offset=nc.lookup_mloc(h1a).addr)
    brow_nm = mem.alloc("brow_nm", [1, T], F32)
    brow_ms = mem.alloc("brow_ms", [1, T], F32)
    brow_t = mem.alloc("brow_t", [1, T], F32)
    brow_rb = mem.alloc("brow_rb", [1, T], BF16)
    brow_mb = mem.alloc("brow_mb", [1, T], BF16)
    rl8 = mem.alloc("rl8", [8, T], F32)
    rsc = mem.alloc("rsc", [128, 256], F32)
    b_end = mem.mark()
    import os as _os
    if _os.environ.get("KMEM_DEBUG"):
        print(f"[mem] arena={arena_bytes} persist={union} a_end={a_end} b_end={b_end} "
              f"slackA={arena_bytes-a_end} slackB={arena_bytes-b_end}")
    assert max(a_end, b_end) <= arena_bytes

    p = [nc.alloc_psum_tensor(f"p{i}", [128, 512], F32) for i in range(6)]
    pX = nc.alloc_psum_tensor("pX", [128, 512], F32)
    pR = nc.alloc_psum_tensor("pR", [1, 512], F32)

    prog = Prog(nc)
    bank_w = {}
    bank_r = {}

    def bw(b):
        return [bank_r[b]] if b in bank_r else []

    def br(b):
        return [bank_w[b]] if b in bank_w else []

    op = prog.op
    op("sync", lambda e: e.dma_start(id_sb[:], id_h.ap()), tag="d_id", dma=True, fam="dset")
    op("sync", lambda e: e.dma_start(hmat[:], hmat_h.ap()), tag="d_hm", dma=True, fam="dset")
    op("sync", lambda e: e.dma_start(sm2[0:8, :], mconst_h.ap()), tag="d_mc", dma=True, fam="dset")
    op("sync", lambda e: e.dma_start(rb8col[:], rb_h.ap().rearrange("(p o) -> p o", o=1)),
       tag="d_rb8", dma=True, fam="dset")
    op("gpsimd", lambda e: e.dma_start(wq[:], Wq_h.ap().rearrange("(kc p) m -> p kc m", p=128)),
       tag="d_wq", dma=True, fam="dwset")
    op("gpsimd", lambda e: e.dma_start(wk[:], Wk_h.ap().rearrange("(kc p) m -> p kc m", p=128)),
       tag="d_wk", dma=True, fam="dwset")
    op("gpsimd", lambda e: e.dma_start(wv[:], Wv_h.ap().rearrange("(kc p) m -> p kc m", p=128)),
       tag="d_wv", dma=True, fam="dwset")
    op("gpsimd", lambda e: e.dma_start(wo[:], Wo_h.ap().rearrange("(kc p) m -> p kc m", p=128)),
       tag="d_wo", dma=True, fam="dwset")
    op("gpsimd", lambda e: e.dma_start(rwsb[:], rw_h.ap().rearrange("(kc p) m -> p kc m", p=128)),
       tag="d_rw", dma=True, fam="dwset")
    op("gpsimd", lambda e: e.dma_start(rb2_sb[:], rb2_h.ap()), tag="d_rb2", dma=True, fam="dwset")
    op("vector", lambda e: e.memset(onescol_f[:], 1.0))
    op("vector", lambda e: e.memset(onescol_b[:], 1.0))
    op("vector", lambda e: e.memset(ones1_f[:], 1.0))
    op("vector", lambda e: e.memset(ones1_b[:], 1.0))
    op("vector", lambda e: e.memset(ones8c[:], 1.0))
    op("vector", lambda e: e.memset(hs32[:], -1e30))
    op("vector", lambda e: e.memset(sm3[:], -1e30), tag="v_ms3")
    op("vector", lambda e: e.memset(cols[0:1, 196:197], 1e-6))
    op("vector", lambda e: e.memset(cols[0:1, 197:198], 1e-5))
    op("vector", lambda e: e.memset(w_hm[:], 0.0))

    smalls = [
        ([(alpha_h, 0, 512), (augp_h, 512, 512), (bq_h, 1024, 512)],
         [(0, 512, C_DEC), (512, 512, C_AUG), (1024, 512, C_BQ)]),
        ([(bk_h, 0, 512), (bv_h, 512, 512), (bo_h, 1024, 1024)],
         [(0, 512, C_BK), (512, 512, C_BV), (1024, 1024, C_BO)]),
        ([(lns_h, 0, 1024), (lnb_h, 1024, 1024)],
         [(0, 1024, C_LNS), (1024, 1024, C_LNB)]),
        ([(sb2_h, 0, 1024)], [(0, 1024, C_SB2)]),
        ([(sb1_h, 0, 2048)], [(0, 2048, C_SB1)]),
    ] + [
        ([(None, 0, 2048)], [(0, 2048, C_RB1 + e * 16)]) for e in range(NE)
    ]
    prev_evac = None
    setup_tidx = 0
    for bi_, (dmas, moves) in enumerate(smalls):
        dtag = f"sd{bi_}"
        for j, (src, roff, ln) in enumerate(dmas):
            ap = (rb1_h.ap()[bi_ - 5] if src is None else src.ap()).flatten()
            wlist = [prev_evac] if (j == 0 and prev_evac) else []
            if bi_ == 1 and j == 0:
                wlist.append("s_sigr")
            op("sync",
               lambda e, ap=ap, roff=roff, ln=ln: e.dma_start(
                   rowstage[:, roff:roff + ln], ap.rearrange("(o f) -> o f", o=1)),
               waits=wlist, tag=f"{dtag}_{j}", dma=True, fam="drow")
        dma_tags = [f"{dtag}_{j}" for j in range(len(dmas))]
        ncols = sum(ln for _, ln, _ in moves) // 128
        colptr = 0
        for roff, ln, coff in moves:
            for jj in range(ln // 128):
                def tp(e, roff=roff, jj=jj, colptr=colptr):
                    return e.transpose(p[0].ap()[:, colptr:colptr + 1],
                                       rowstage[0:1, roff + jj * 128:roff + (jj + 1) * 128],
                                       id_sb[0:1, 0:1])
                w = dma_tags + ["d_id"] + (bw(0) if (colptr == 0) else [])
                op("tensor", tp, waits=w,
                   tag=f"st{setup_tidx}" if (jj == ln // 128 - 1 and (roff, ln, coff) == moves[-1]) else None)
                colptr += 1
        bank_w[0] = f"st{setup_tidx}"
        colptr = 0
        last = None
        for mi, (roff, ln, coff) in enumerate(moves):
            n = ln // 128
            def ev(e, colptr=colptr, n=n, coff=coff):
                return e.tensor_copy(cols[:, coff:coff + n], p[0].ap()[:, colptr:colptr + n])
            last = f"se{setup_tidx}_{mi}"
            op("vector", ev, waits=[f"st{setup_tidx}"], tag=last)
            colptr += n
        bank_r[0] = last
        prev_evac = last
        setup_tidx += 1
        if bi_ == 0:
            op("scalar", lambda e: e.activation(sm1[0:1, 0:512], rowstage[0:1, 0:512],
                                                AF.Sigmoid),
               waits=["sd0_0", "v_ms3"], tag="s_sigr")
            op("vector", lambda e: e.tensor_scalar(sm1[0:1, 0:512], sm1[0:1, 0:512], -1.0, 1.0,
                                                   ALU.mult, ALU.add), waits=["s_sigr"],
               tag="v_omd")
            op("tensor", lambda e: e.matmul(p[2][:], ones1_f[:], sm1[0:1, 0:512],
                                            start=True, stop=True),
               waits=["v_omd"] + bw(2), tag="t_omb")
            bank_w[2] = "t_omb"
            op("vector", lambda e: e.tensor_tensor(B_sb[:], sm2[0:8, :], p[2].ap()[0:8, :],
                                                   ALU.mult),
               waits=["t_omb", "d_mc"], tag="v_bsb")
            bank_r[2] = "v_bsb"
            op("scalar", lambda e: e.activation(cols[:, C_DEC:C_DEC + 4],
                                                cols[:, C_DEC:C_DEC + 4], AF.Sigmoid),
               waits=["se0_0"], tag="s_dec")

    op("vector", lambda e: e.tensor_copy(o2_bf.ap()[:, 0, 0:128], id_sb[:]),
       waits=["d_id"], tag="s_idb")
    for kc in range(4):
        for mb in range(8):
            def tpw(e, kc=kc, mb=mb):
                return e.transpose(
                    p[3].ap().bitcast(BF16)[:, (mb % 4) * 128:(mb % 4) * 128 + 128],
                    wo.ap()[:, kc, mb * 128:(mb + 1) * 128],
                    o2_bf.ap()[:, 0, 0:128])
            w = (["d_wo", "s_idb"] if (kc == 0 and mb == 0) else []) + \
                (bw(3) if mb % 4 == 0 else [])
            op("tensor", tpw, waits=w, tag=f"s_wt{kc}_{mb}" if mb % 4 == 3 else None)
            if mb % 4 == 3:
                bank_w[3] = f"s_wt{kc}_{mb}"
                op("vector", lambda e, kc=kc, mb=mb: e.tensor_copy(
                    u_y.ap()[:, mb // 4, :], p[3].ap().bitcast(BF16)[:, 0:512]),
                   waits=[f"s_wt{kc}_{mb}"], tag=f"s_we{kc}_{mb}")
                bank_r[3] = f"s_we{kc}_{mb}"
        for mb in range(8):
            def msum(e, kc=kc, mb=mb):
                return e.matmul(p[3].ap()[:, 0:1],
                                u_y.ap()[:, mb // 4, (mb % 4) * 128:(mb % 4) * 128 + 128],
                                onescol_b[:], start=(mb == 0), stop=(mb == 7))
            w = ([f"s_we{kc}_7"] + bw(3)) if mb == 0 else []
            op("tensor", msum, waits=w, tag=f"s_ws{kc}" if mb == 7 else None)
        bank_w[3] = f"s_ws{kc}"
        op("vector", lambda e, kc=kc: e.tensor_copy(wosum.ap()[:, kc:kc + 1],
                                                    p[3].ap()[:, 0:1]),
           waits=[f"s_ws{kc}"], tag=f"s_wc{kc}")
        bank_r[3] = f"s_wc{kc}"
    op("tensor", lambda e: e.matmul(pR.ap()[0:1, 0:8], onescol_f[:],
                                    cols.ap()[:, C_BO:C_BO + 8], start=True, stop=True),
       waits=["se1_2"] + bw("R"), tag="s_bo1")
    bank_w["R"] = "s_bo1"
    op("vector", lambda e: e.tensor_copy(bosum[:], pR.ap()[0:1, 0:8]),
       waits=["s_bo1"], tag="s_bo2")
    bank_r["R"] = "s_bo2"
    op("vector", lambda e: e.tensor_tensor(bosum.ap()[:, 0:4], bosum.ap()[:, 0:4],
                                           bosum.ap()[:, 4:8], ALU.add),
       waits=["s_bo2"], tag="s_bo3", force=True)
    op("vector", lambda e: e.tensor_tensor(bosum.ap()[:, 0:2], bosum.ap()[:, 0:2],
                                           bosum.ap()[:, 2:4], ALU.add),
       waits=["s_bo3"], tag="s_bo4", force=True)
    op("vector", lambda e: e.tensor_tensor(bosum.ap()[:, 0:1], bosum.ap()[:, 0:1],
                                           bosum.ap()[:, 1:2], ALU.add),
       waits=["s_bo4"], tag="s_bo5", force=True)

    prog.barrier()

    if stop_after == "setup":
        op("sync", None, waits=["d_rb2", "d_id"])
        emit_all(nc, prog)
        return nc

    dec_ap = lambda c: cols.ap()[:, C_DEC + c:C_DEC + c + 1]

    def dump(name, apf, wtags, stall_eng="vector"):
        if name in dbg_t:
            op("sync", lambda e, apf=apf, name=name: e.dma_start(dbg_t[name].ap(), apf()),
               waits=wtags, tag="dbg_" + name, dma=True, fam="dbg_" + name)
            op(stall_eng, None, waits=["dbg_" + name])

    rot = [0, 1]
    rotu = [4, 5]
    state = {"gbig": 0, "gu": 0}

    def emit_front(i):
        pf = i % 2
        wl = [f"ttx{i-1}_7"] if i > 0 else []
        op("sync", lambda e, i=i: e.dma_start(
            x_tm[:], x_hbm.ap()[i * T:(i + 1) * T, :].rearrange("(g p) d -> p g d", p=128)),
            waits=wl, tag=f"dx{i}", dma=True, fam="dx")
        for c in range(8):
            b = rot[state["gbig"] % 2]
            for g in range(4):
                def tp(e, c=c, g=g, b=b, i=i):
                    return e.transpose(p[b].ap()[:, g * 128:(g + 1) * 128],
                                       x_tm.ap()[:, g, c * 128:(c + 1) * 128], id_sb[:])
                w = ([f"dx{i}"] if (c == 0 and g == 0) else []) + (bw(b) if g == 0 else [])
                op("tensor", tp, waits=w, tag=f"ttx{i}_{c}" if g == 3 else None)
            bank_w[b] = f"ttx{i}_{c}"
            wl2 = [f"ttx{i}_{c}"] + ([f"vz{i-2}"] if (i > 1 and c == 0) else [])
            op("scalar", lambda e, c=c, b=b, pf=pf: e.copy(x_fm[pf].ap()[:, c, :], p[b].ap()),
               waits=wl2, tag=f"sxf{i}_{c}")
            wl2b = [f"tq{i-1}_v_3"] if (i > 0 and c == 0) else []
            op("scalar", lambda e, c=c, b=b: e.copy(x_bf.ap()[:, c, :], p[b].ap()),
               waits=wl2b, tag=f"gxb{i}_{c}")
            bank_r[b] = f"gxb{i}_{c}"
            state["gbig"] += 1
        for wname, wsb, dstl, bcol in (
                ("q", wq, q_sb, C_BQ), ("k", wk, k_sb, C_BK), ("v", wv, v_sb, C_BV)):
            for mc in range(4):
                b = rot[state["gbig"] % 2]
                for kc in range(8):
                    def mm(e, b=b, kc=kc, mc=mc, wsb=wsb):
                        return e.matmul(p[b].ap(), wsb.ap()[:, kc, mc * 128:(mc + 1) * 128],
                                        x_bf.ap()[:, kc, :], start=(kc == 0), stop=(kc == 7))
                    w = bw(b) if kc == 0 else []
                    if kc == 0 and mc == 0 and wname == "q":
                        w = w + [f"gxb{i}_{c}" for c in range(8)] + (["d_rb2"] if i == 0 else [])
                    op("tensor", mm, waits=w, tag=f"tq{i}_{wname}_{mc}" if kc == 7 else None)
                bank_w[b] = f"tq{i}_{wname}_{mc}"
                wl2 = [f"tq{i}_{wname}_{mc}"]
                if i > 1 and mc == 0:
                    wl2 += {"q": [f"vin{i-2}_3"], "k": [f"vsq{i-2}_3"], "v": [f"vu{i-2}_3"]}[wname]
                def ev(e, b=b, mc=mc, dstl=dstl, bcol=bcol, pf=pf):
                    return e.activation(dstl[pf].ap()[:, mc, :], p[b].ap(), AF.Identity,
                                        bias=cols.ap()[:, bcol + mc:bcol + mc + 1])
                op("scalar", ev, waits=wl2, tag=f"sq{i}_{wname}_{mc}")
                bank_r[b] = f"sq{i}_{wname}_{mc}"
                state["gbig"] += 1

    def butterfly(srcs, dst, tmp, opk, tagp, src_tag):
        prev = src_tag
        for r, x in enumerate((4, 2, 1)):
            mask = [(j ^ x) % 32 for j in range(32)]
            sa = srcs if r == 0 else dst
            t1 = f"{tagp}_s{r}"
            op("vector", lambda e, sa=sa, tmp=tmp, mask=mask: e.stream_shuffle(
                tmp[:], sa[:], mask), waits=[prev], tag=t1, force=True)
            t2 = tagp if r == 2 else f"{tagp}_m{r}"
            op("vector", lambda e, sa=sa, dst=dst, tmp=tmp: e.tensor_tensor(
                dst[:], sa[:], tmp[:], opk), waits=[t1], tag=t2, force=True)
            prev = t2

    def emit_attn(i):
        pf = i % 2
        ti = i % 2
        if a_stage == "qkv":
            return
        for c in range(4):
            wl2 = [f"sq{i}_q_{c}", f"sq{i}_k_{c}"]
            op("vector", lambda e, c=c, pf=pf: e.tensor_tensor(
                sqk.ap()[:, c, :], q_sb[pf].ap()[:, c, :], k_sb[pf].ap()[:, c, :], ALU.mult),
                waits=wl2, tag=f"vsq{i}_{c}")
        for c in range(4):
            def mm(e, c=c):
                return e.matmul(pX.ap()[0:8, :], hmat.ap()[:, c, :], sqk.ap()[:, c, :],
                                start=(c == 0), stop=(c == 3))
            w = [f"vsq{i}_{c}"] + (["d_hm"] if i == 0 and c == 0 else []) + (bw("X") if c == 0 else [])
            op("tensor", mm, waits=w, tag=f"ths{i}" if c == 3 else None)
        bank_w["X"] = f"ths{i}"
        op("vector", lambda e: e.tensor_copy(hs32[0:8, :], pX.ap()[0:8, :]),
           waits=[f"ths{i}"], tag=f"vhs{i}")
        bank_r["X"] = f"vhs{i}"
        butterfly(hs32, sm1, sm2, ALU.max, f"vbm{i}", f"vhs{i}")
        op("vector", lambda e: e.tensor_tensor(sm2[0:8, :], hs32[0:8, :], sm1[0:8, :],
                                               ALU.subtract), tag=f"vsub{i}")
        op("scalar", lambda e: e.activation(sm2[0:8, :], sm2[0:8, :], AF.Exp),
           waits=[f"vsub{i}"], tag=f"sexp{i}")
        op("tensor", lambda e: e.matmul(pX.ap()[0:1, :], ones8c[:], sm2[0:8, :],
                                        start=True, stop=True),
           waits=[f"sexp{i}"] + bw("X"), tag=f"tz{i}")
        bank_w["X"] = f"tz{i}"
        op("scalar", lambda e: e.activation(arow_t[:], pX.ap()[0:1, :], AF.Ln), waits=[f"tz{i}"])
        op("scalar", lambda e: e.activation(arow_rb[:], arow_t[:], AF.Exp, scale=-1.0),
           tag=f"vrz{i}")
        bank_r["X"] = f"vrz{i}"
        op("tensor", lambda e: e.matmul(pX.ap(), ones1_b[:], arow_rb[:], start=True, stop=True),
           waits=[f"vrz{i}"] + bw("X"), tag=f"tzb{i}")
        bank_w["X"] = f"tzb{i}"
        op("vector", lambda e: e.tensor_tensor(attn_bf[:], sm2[0:8, :], pX.ap()[0:8, :],
                                               ALU.mult), waits=[f"tzb{i}"], tag=f"vat{i}")
        bank_r["X"] = f"vat{i}"
        for c in range(4):
            b = rotu[state["gu"] % 2]
            def mm(e, c=c, b=b):
                return e.matmul(p[b].ap(), B_sb.ap()[:, c * 128:(c + 1) * 128], attn_bf[:],
                                start=True, stop=True)
            w = [f"vat{i}"] + bw(b) + (["v_bsb"] if i == 0 and c == 0 else [])
            op("tensor", mm, waits=w, tag=f"tu{i}_{c}")
            bank_w[b] = f"tu{i}_{c}"
            wl2 = [f"tu{i}_{c}", f"sq{i}_v_{c}"]
            op("vector", lambda e, c=c, b=b: e.tensor_tensor(
                u_y.ap()[:, c, :], p[b].ap(), v_sb[pf].ap()[:, c, :], ALU.mult),
                waits=wl2, tag=f"vu{i}_{c}")
            bank_r[b] = f"vu{i}_{c}"
            state["gu"] += 1
            init = 0.0 if i == 0 else cols.ap()[:, 198 + c:199 + c]
            op("vector", lambda e, c=c, init=init: e.tensor_tensor_scan(
                u_y.ap()[:, c, :], dec_ap(c).broadcast_to([128, T]),
                u_y.ap()[:, c, :], init, ALU.mult, ALU.add),
                waits=(["s_dec"] if i == 0 and c == 0 else []), tag=None)
            op("vector", lambda e, c=c: e.tensor_copy(
                cols.ap()[:, 198 + c:199 + c], u_y.ap()[:, c, T - 1:T]),
                tag=f"vsc{i}_{c}")
        if a_stage == "scan":
            return
    def emit_wl1(i):
        pf = i % 2
        ti = i % 2
        if a_stage in ("qkv", "scan"):
            return
        for c in range(4):
            wl2 = [f"sq{i}_q_{c}", f"vin{i}_{c-1}" if c > 0 else (f"vin{i-1}_3" if i > 0 else f"vsc{i}_0")]
            op("scalar", lambda e, c=c, pf=pf: e.activation(
                aug_bf.ap()[:, 0, :], q_sb[pf].ap()[:, c, :], AF.Sigmoid,
                scale=cols.ap()[:, C_AUG + c:C_AUG + c + 1]), waits=wl2, tag=f"saug{i}_{c}")
            wl2 = [f"vsc{i}_{c}", f"saug{i}_{c}"] + ([f"two{i-1}_7"] if (i > 0 and c == 0) else [])
            op("vector", lambda e, c=c, pf=pf: e.tensor_tensor(
                itmp[:], q_sb[pf].ap()[:, c, :], u_y.ap()[:, c, :], ALU.mult), waits=wl2)
            op("vector", lambda e, c=c: e.tensor_tensor(
                inner_bf.ap()[:, c, :], itmp[:], aug_bf.ap()[:, 0, :], ALU.add),
                tag=f"vin{i}_{c}")
        for kc in range(4):
            def mm(e, kc=kc):
                return e.matmul(pR.ap(), wosum.ap()[:, kc:kc + 1], inner_bf.ap()[:, kc, :],
                                start=(kc == 0), stop=(kc == 3))
            w = (bw("R") + [f"vin{i}_3"]) if kc == 0 else []
            op("tensor", mm, waits=w, tag=f"tm1_{i}" if kc == 3 else None)
        bank_w["R"] = f"tm1_{i}"
        op("vector", lambda e: e.tensor_scalar(row_nm[:], pR.ap(), bosum.ap()[0:1, 0:1],
                                               -1.0 / D, ALU.add, ALU.mult),
           waits=[f"tm1_{i}"], tag=f"vnm{i}")
        bank_r["R"] = f"vnm{i}"
        for mc in range(8):
            b = rotu[state["gu"] % 2]
            for kc in range(4):
                def mm(e, b=b, kc=kc, mc=mc):
                    return e.matmul(p[b].ap(), wo.ap()[:, kc, mc * 128:(mc + 1) * 128],
                                    inner_bf.ap()[:, kc, :], start=(kc == 0), stop=(kc == 3))
                w = (bw(b) if kc == 0 else []) + ([f"vin{i}_3"] if (kc == 0 and mc == 0) else [])
                op("tensor", mm, waits=w, tag=f"two{i}_{mc}" if kc == 3 else None)
            bank_w[b] = f"two{i}_{mc}"
            wl2 = [f"two{i}_{mc}"] + ([f"dh32_{i-1}", f"gh{i-1}"] if (i > 0 and mc == 0) else [])
            op("scalar", lambda e, b=b, mc=mc: e.activation(
                bufo.ap()[:, mc, :], p[b].ap(), AF.Identity,
                bias=cols.ap()[:, C_BO + mc:C_BO + mc + 1]), waits=wl2, tag=f"so{i}_{mc}")
            bank_r[b] = f"so{i}_{mc}"
            state["gu"] += 1
        if i == 0:
            dump("o", lambda: bufo.ap(), [f"so{i}_{mc}" for mc in range(8)], stall_eng="scalar")
        if a_stage == "wo":
            return
        for mc in range(8):
            wl2 = [f"so{i}_{mc}"] + ([f"tsq1_{i-1}"] if (i > 0 and mc == 0) else [])
            op("scalar", lambda e, mc=mc: e.activation(
                o2_bf.ap()[:, mc, :], bufo.ap()[:, mc, :], AF.Square), waits=wl2,
                tag=f"so2{i}_{mc}")
        for mc in range(8):
            def mm(e, mc=mc):
                return e.matmul(pR.ap(), onescol_b[:], o2_bf.ap()[:, mc, :],
                                start=(mc == 0), stop=(mc == 7))
            w = [f"so2{i}_{mc}"] + (bw("R") if mc == 0 else [])
            op("tensor", mm, waits=w, tag=f"tsq1_{i}" if mc == 7 else None)
        bank_w["R"] = f"tsq1_{i}"
        op("vector", lambda e: e.tensor_scalar(row_ms[:], pR.ap(), 1.0 / D, None, ALU.mult),
           waits=[f"tsq1_{i}"], tag=f"vms{i}")
        bank_r["R"] = f"vms{i}"
        op("vector", lambda e: e.tensor_tensor(row_t[:], row_nm[:], row_nm[:], ALU.mult))
        op("vector", lambda e: e.tensor_tensor(row_t[:], row_ms[:], row_t[:], ALU.subtract),
           tag=f"vvar{i}")
        op("scalar", lambda e: e.activation(row_t[:], row_t[:], AF.Ln, bias=cols.ap()[0:1, 196:197]),
           waits=[f"vvar{i}"])
        op("scalar", lambda e: e.activation(row_rb[:], row_t[:], AF.Exp, scale=-0.5),
           tag=f"ssd{i}")
        op("vector", lambda e: e.tensor_tensor(row_mb[:], row_nm[:], row_rb[:], ALU.mult),
           waits=[f"ssd{i}"], tag=f"vr{i}")
        op("tensor", lambda e: e.matmul(p[2].ap(), ones1_b[:], row_rb[:], start=True, stop=True),
           waits=[f"vr{i}"] + bw(2), tag=None)
        op("tensor", lambda e: e.matmul(p[3].ap(), ones1_b[:], row_mb[:], start=True, stop=True),
           waits=bw(3), tag=f"tbc1_{i}")
        bank_w[2] = f"tbc1_{i}"
        bank_w[3] = f"tbc1_{i}"
        for mc in range(8):
            wl2 = ([f"tbc1_{i}"] if mc == 0 else []) + ([f"sz2_{i-1}_7"] if (i > 0 and mc == 0) else [])
            op("vector", lambda e, mc=mc: e.tensor_tensor(
                c1.ap()[:, mc, :], bufo.ap()[:, mc, :], p[2].ap(), ALU.mult), waits=wl2)
            op("vector", lambda e, mc=mc: e.tensor_tensor(
                bufo.ap()[:, mc, :], c1.ap()[:, mc, :], p[3].ap(), ALU.add),
                tag=f"vln1_{i}" if mc == 7 else None)
        for mc in range(8):
            op("vector", lambda e, mc=mc: e.tensor_scalar(
                c1.ap()[:, mc, :], bufo.ap()[:, mc, :],
                cols.ap()[:, C_LNS + mc:C_LNS + mc + 1],
                cols.ap()[:, C_LNB + mc:C_LNB + mc + 1], ALU.mult, ALU.add))
            op("vector", lambda e, mc=mc, pf=pf: e.tensor_tensor(
                c1.ap()[:, mc, :], c1.ap()[:, mc, :], x_fm[pf].ap()[:, mc, :], ALU.add),
                tag=f"vz{i}" if mc == 7 else None)
    def emit_back2(i):
        pf = i % 2
        ti = i % 2
        if a_stage in ("qkv", "scan", "wo"):
            return
        for mc in range(8):
            wl2 = [f"vz{i}"] + ([f"tsq1_{i}"] if mc == 0 else [])
            op("scalar", lambda e, mc=mc: e.activation(
                o2_bf.ap()[:, mc, :], c1.ap()[:, mc, :], AF.Square), waits=wl2,
                tag=f"sz2_{i}_{mc}")
        for mc in range(8):
            def mm(e, mc=mc):
                return e.matmul(pR.ap(), onescol_f[:], c1.ap()[:, mc, :],
                                start=(mc == 0), stop=(mc == 7))
            w = ([f"vz{i}"] + bw("R")) if mc == 0 else []
            op("tensor", mm, waits=w, tag=f"tm2_{i}" if mc == 7 else None)
        bank_w["R"] = f"tm2_{i}"
        op("vector", lambda e: e.tensor_scalar(row_nm[:], pR.ap(), -1.0 / D, None, ALU.mult),
           waits=[f"tm2_{i}"], tag=f"vnm2{i}")
        bank_r["R"] = f"vnm2{i}"
        for mc in range(8):
            def mm(e, mc=mc):
                return e.matmul(pR.ap(), onescol_b[:], o2_bf.ap()[:, mc, :],
                                start=(mc == 0), stop=(mc == 7))
            w = [f"sz2_{i}_{mc}"] + (bw("R") if mc == 0 else [])
            op("tensor", mm, waits=w, tag=f"tsq2_{i}" if mc == 7 else None)
        bank_w["R"] = f"tsq2_{i}"
        op("vector", lambda e: e.tensor_scalar(row_ms[:], pR.ap(), 1.0 / D, None, ALU.mult),
           waits=[f"tsq2_{i}"], tag=f"vms2{i}")
        bank_r["R"] = f"vms2{i}"
        op("vector", lambda e: e.tensor_tensor(row_t[:], row_nm[:], row_nm[:], ALU.mult))
        op("vector", lambda e: e.tensor_tensor(row_t[:], row_ms[:], row_t[:], ALU.subtract),
           tag=f"vvar2{i}")
        op("scalar", lambda e: e.activation(row_t[:], row_t[:], AF.Ln, bias=cols.ap()[0:1, 197:198]),
           waits=[f"vvar2{i}"])
        op("scalar", lambda e: e.activation(row_rb[:], row_t[:], AF.Exp, scale=-0.5),
           tag=f"ssd2{i}")
        op("vector", lambda e: e.tensor_tensor(row_mb[:], row_nm[:], row_rb[:], ALU.mult),
           waits=[f"ssd2{i}"], tag=f"vr2{i}")
        wl2 = [f"vr2{i}", f"vln1_{i}"]
        op("tensor", lambda e: e.matmul(p[2].ap(), ones1_b[:], row_rb[:], start=True, stop=True),
           waits=wl2, tag=None)
        op("tensor", lambda e: e.matmul(p[3].ap(), ones1_b[:], row_mb[:], start=True, stop=True),
           waits=[], tag=f"tbc2_{i}")
        bank_w[2] = f"tbc2_{i}"
        bank_w[3] = f"tbc2_{i}"
        for mc in range(8):
            wl2 = [f"tbc2_{i}"] if mc == 0 else []
            op("vector", lambda e, mc=mc: e.tensor_tensor(
                bufo.ap()[:, mc, :], c1.ap()[:, mc, :], p[2].ap(), ALU.mult), waits=wl2)
            op("vector", lambda e, mc=mc: e.tensor_tensor(
                bufo.ap()[:, mc, :], bufo.ap()[:, mc, :], p[3].ap(), ALU.add),
                tag=f"vh{i}" if mc == 7 else None)
        bank_r[2] = f"vh{i}"
        bank_r[3] = f"vh{i}"
        for mc in range(8):
            wl2 = [f"vh{i}"] + ([f"dh16_{i-1}"] if (i > 0 and mc == 0) else [])
            op("scalar", lambda e, mc=mc: e.copy(h_bf.ap()[:, mc, :], bufo.ap()[:, mc, :]),
               waits=wl2, tag=f"gh{i}" if mc == 7 else None)
        op("sync", lambda e, i=i: e.dma_start(
            h32_hbm.ap()[:, :, i * T:(i + 1) * T].rearrange("c p t -> p c t"), bufo.ap()),
            waits=[f"vh{i}"], tag=f"dh32_{i}", dma=True, fam="dh32")
        op("sync", lambda e, i=i: e.dma_start(
            h16_hbm.ap()[:, :, i * T:(i + 1) * T].rearrange("c p t -> p c t"), h_bf.ap()),
            waits=[f"gh{i}"], tag=f"dh16_{i}", dma=True, fam="dh16")
        if i == 0:
            dump("h", lambda: bufo.ap(), [f"vh{i}"], stall_eng="scalar")
            dump("q", lambda: q_sb[0].ap(), [f"sq{i}_q_3"], stall_eng="scalar")
            dump("attn", lambda: attn_bf.ap(), [f"vat{i}"])
            dump("st", lambda: u_y.ap(), [f"vsc{i}_3"])

    emit_front(0)
    emit_attn(0)
    emit_wl1(0)
    for i in range(n_tiles):
        if i + 1 < n_tiles:
            emit_front(i + 1)
        emit_back2(i)
        if i + 1 < n_tiles:
            emit_attn(i + 1)
            emit_wl1(i + 1)

    W1slots = [W1, WP]
    passes = []
    for _q in range(NQ):
        _pl = ([-1] + list(range(b_experts))) if b_experts > 0 else [-1]
        passes += [(_q, _e) for _e in _pl]
    nP = len(passes)

    def w1_src(ei):
        return (sw1_h.ap() if ei < 0 else rw1_h.ap()[ei]).rearrange("(kc p) m -> p kc m", p=128)

    def w2_src(ei):
        return (sw2_h.ap() if ei < 0 else rw2_h.ap()[ei]).rearrange("(kc p) m -> p kc m", p=128)

    op("gpsimd", lambda e: e.dma_start(W1slots[0].ap(), w1_src(passes[0][1])),
       waits=[f"sq{n_tiles-1}_v_3"], tag="dW1_0", dma=True, fam="dw1")
    op("gpsimd", lambda e: e.dma_start(W2.ap(), w2_src(passes[0][1])),
       waits=[f"two{n_tiles-1}_7"], tag="dW2_0", dma=True, fam="dw2")

    prog.barrier()

    if stop_after == "A":
        lt = n_tiles - 1
        wmap = {"qkv": [f"sq{lt}_v_3"], "scan": [f"vsc{lt}_3"], "wo": [f"so{lt}_7"],
                "ln": [f"vh{lt}"], "all": [f"dh32_{lt}", f"dh16_{lt}"]}
        for eng in ("sync",):
            op(eng, None, waits=wmap[a_stage])
        emit_all(nc, prog)
        return nc

    h1_prev2 = {}
    pb_rot1 = [0, 1]
    pb_rot2 = [2, 3]
    g1 = 0
    g2 = 0
    step = 0

    op("gpsimd", lambda e: e.dma_start(W1slots[1].ap(), w1_src(passes[1][1])),
       waits=[f"dh32_{NT-1}"], tag="dW1_1", dma=True, fam="dw1")

    laste_g = str(b_experts - 1) if b_experts > 0 else "s"

    def emit_dhb(q2):
        t0 = q2 * QT
        for st in range(NST):
            wl = [f"tmm1_{q2-1}_{laste_g}_{st}_15"] if q2 > 0 else [f"dh16_{NT-1}"]
            op("sync", lambda e, t0=t0, st=st: e.dma_start(
                hbB.ap()[:, :, st * T:(st + 1) * T],
                h16_hbm.ap()[:, :, t0 + st * T:t0 + (st + 1) * T].rearrange("c p t -> p c t")),
                waits=wl, tag=f"dhb{q2}_{st}", dma=True, fam="dhb")

    def emit_dacc(q2):
        t0 = q2 * QT
        for mc in range(8):
            wl = [f"tt3_{q2-1}_1_3_{mc // 4}"] if q2 > 0 else [f"dh32_{NT-1}"]
            op("sync", lambda e, t0=t0, mc=mc: e.dma_start(
                acc.ap()[:, mc, :], h32_hbm.ap()[mc, :, t0:t0 + QT]),
                waits=wl, tag=f"dacc{q2}_{mc}", dma=True, fam="dacc")

    rt_p2 = {}

    def emit_router(q2):
        t0 = q2 * QT
        for st in range(NST):
            tk = st * T
            for kc in range(8):
                def mm(e, kc=kc, tk=tk):
                    return e.matmul(pX.ap()[0:8, :], rwsb.ap()[:, kc, :],
                                    hbB.ap()[:, kc, tk:tk + T], start=(kc == 0), stop=(kc == 7))
                w = ([f"dhb{q2}_{st}"] + bw("X")) if kc == 0 else []
                op("tensor", mm, waits=w, tag=f"trtB_{q2}_{st}" if kc == 7 else None)
            bank_w["X"] = f"trtB_{q2}_{st}"
            tg = [f"trtB_{q2}_{st}"]

            def chain(eng, fn, extra=(), tagn=None):
                t = tagn or f"rt{q2}_{st}_{chain.n}"
                chain.n += 1
                op(eng, fn, waits=([tg[0]] if tg[0] else []) + list(extra), tag=t, force=True)
                tg[0] = t
            chain.n = 0
            chain("vector", lambda e: e.tensor_scalar(rl8[:], pX.ap()[0:8, :], rb8col.ap(),
                                                      None, ALU.add), extra=["d_rb8"])
            bank_r["X"] = tg[0]
            for j in range(4):
                def tp(e, j=j):
                    return e.transpose(pX.ap()[:, j * 8:(j + 1) * 8],
                                       rl8.ap()[:, j * 128:(j + 1) * 128], id_sb[0:8, 0:8])
                chain("tensor", tp)
            bank_w["X"] = tg[0]
            chain("vector", lambda e: e.tensor_copy(rsc.ap()[:, 0:32], pX.ap()[:, 0:32]))
            bank_r["X"] = tg[0]
            v3 = lambda off: rsc.ap()[:, off:off + 32].rearrange("p (g x) -> p g x", g=4)
            chain("vector", lambda e: e.tensor_tensor(v3(96)[:, :, 0:4], v3(0)[:, :, 0:4],
                                                      v3(0)[:, :, 4:8], ALU.max))
            chain("vector", lambda e: e.tensor_tensor(v3(96)[:, :, 0:2], v3(96)[:, :, 0:2],
                                                      v3(96)[:, :, 2:4], ALU.max))
            chain("vector", lambda e: e.tensor_tensor(
                rsc.ap()[:, 160:164].rearrange("p (g x) -> p g x", x=1),
                v3(96)[:, :, 0:1], v3(96)[:, :, 1:2], ALU.max))
            for g in range(4):
                chain("vector", lambda e, g=g: e.tensor_scalar(
                    rsc.ap()[:, 32 + g * 8:40 + g * 8], rsc.ap()[:, g * 8:g * 8 + 8],
                    rsc.ap()[:, 160 + g:161 + g], None, ALU.subtract))
            chain("scalar", lambda e: e.activation(rsc.ap()[:, 32:64], rsc.ap()[:, 32:64],
                                                   AF.Exp))
            chain("vector", lambda e: e.tensor_tensor(v3(128)[:, :, 0:4], v3(32)[:, :, 0:4],
                                                      v3(32)[:, :, 4:8], ALU.add))
            chain("vector", lambda e: e.tensor_tensor(v3(128)[:, :, 0:2], v3(128)[:, :, 0:2],
                                                      v3(128)[:, :, 2:4], ALU.add))
            chain("vector", lambda e: e.tensor_tensor(
                rsc.ap()[:, 168:172].rearrange("p (g x) -> p g x", x=1),
                v3(128)[:, :, 0:1], v3(128)[:, :, 1:2], ALU.add))
            chain("scalar", lambda e: e.activation(rsc.ap()[:, 168:172], rsc.ap()[:, 168:172],
                                                   AF.Ln))
            chain("scalar", lambda e: e.activation(rsc.ap()[:, 168:172], rsc.ap()[:, 168:172],
                                                   AF.Exp, scale=-1.0))
            for g in range(4):
                chain("vector", lambda e, g=g: e.tensor_scalar(
                    rsc.ap()[:, 32 + g * 8:40 + g * 8], rsc.ap()[:, 32 + g * 8:40 + g * 8],
                    rsc.ap()[:, 168 + g:169 + g], None, ALU.mult))
            for g in range(4):
                chain("vector", lambda e, g=g: e.tensor_scalar(
                    rsc.ap()[:, 64 + g * 8:72 + g * 8], rsc.ap()[:, g * 8:g * 8 + 8],
                    rsc.ap()[:, 160 + g:161 + g], None, ALU.is_ge))
            chain("vector", lambda e: e.scalar_tensor_tensor(
                rsc.ap()[:, 96:128], rsc.ap()[:, 64:96], -1e30, rsc.ap()[:, 0:32],
                ALU.mult, ALU.add))
            chain("vector", lambda e: e.tensor_tensor(v3(128)[:, :, 0:4], v3(96)[:, :, 0:4],
                                                      v3(96)[:, :, 4:8], ALU.max))
            chain("vector", lambda e: e.tensor_tensor(v3(128)[:, :, 0:2], v3(128)[:, :, 0:2],
                                                      v3(128)[:, :, 2:4], ALU.max))
            chain("vector", lambda e: e.tensor_tensor(
                rsc.ap()[:, 164:168].rearrange("p (g x) -> p g x", x=1),
                v3(128)[:, :, 0:1], v3(128)[:, :, 1:2], ALU.max))
            for g in range(4):
                chain("vector", lambda e, g=g: e.tensor_scalar(
                    rsc.ap()[:, 128 + g * 8:136 + g * 8], rsc.ap()[:, 96 + g * 8:104 + g * 8],
                    rsc.ap()[:, 164 + g:165 + g], None, ALU.is_ge))
            chain("vector", lambda e: e.tensor_tensor(rsc.ap()[:, 64:96], rsc.ap()[:, 64:96],
                                                      rsc.ap()[:, 128:160], ALU.add))
            wbase = 192 + st * 32
            chain("vector", lambda e, wbase=wbase: e.tensor_tensor(
                rsc.ap()[:, wbase:wbase + 32], rsc.ap()[:, 32:64],
                rsc.ap()[:, 64:96], ALU.mult), tagn=f"vwt_{q2}_{st}")

            def part2(st=st, tk=tk, wbase=wbase):
                tgl = [f"vwt_{q2}_{st}"]
                for j in range(4):
                    def tpb(e, j=j):
                        return e.transpose(p[5].ap()[0:8, j * 128:(j + 1) * 128],
                                           rsc.ap()[:, wbase + j * 8:wbase + j * 8 + 8], id_sb[:])
                    w = tgl + (bw(5) if j == 0 else [])
                    tgl = []
                    op("tensor", tpb, waits=w, tag=f"twT_{q2}_{st}" if j == 3 else None)
                bank_w[5] = f"twT_{q2}_{st}"
                op("vector", lambda e, tk=tk, tok0=t0: e.tensor_copy(
                    w_hm.ap()[0:8, tok0 + tk:tok0 + tk + T], p[5].ap()[0:8, :]),
                    waits=[f"twT_{q2}_{st}"], tag=f"vw_{q2}_{st}", force=True)
                bank_r[5] = f"vw_{q2}_{st}"
            rt_p2[(q2, st)] = part2

    gP = 0
    dout_prev = {}
    for qq in range(NQ):
        tok0 = qq * QT
        if qq == 0:
            emit_dhb(0)
            emit_dacc(0)
            emit_router(0)

        def ln3_sq(st):
            tk = st * T
            laste = str(b_experts - 1) if b_experts > 0 else "s"
            for mc in range(8):
                op("scalar", lambda e, mc=mc, tk=tk: e.activation(
                    sq_bf.ap()[:, mc, :], acc.ap()[:, mc, tk:tk + T], AF.Square),
                    waits=[f"vac_{qq}_{laste}_{st}_7"] if mc == 0 else
                          [f"vac_{qq}_{laste}_{st}_{mc}"],
                    tag=f"sz3_{qq}_{st}_{mc}")

        def ln3_stats_p1(st, sq_done=False):
            tk = st * T
            laste = str(b_experts - 1) if b_experts > 0 else "s"
            if not sq_done:
                ln3_sq(st)
            for mc in range(8):
                def mm(e, mc=mc, tk=tk):
                    return e.matmul(pR.ap(), onescol_f[:], acc.ap()[:, mc, tk:tk + T],
                                    start=(mc == 0), stop=(mc == 7))
                w = [f"vac_{qq}_{laste}_{st}_{mc}"] + (bw("R") if mc == 0 else [])
                op("tensor", mm, waits=w, tag=f"tm3_{qq}_{st}" if mc == 7 else None)
            bank_w["R"] = f"tm3_{qq}_{st}"
            op("vector", lambda e: e.tensor_scalar(brow_nm[:], pR.ap(), -1.0 / D, None,
                                                   ALU.mult),
               waits=[f"tm3_{qq}_{st}"], tag=f"vnm3_{qq}_{st}")
            bank_r["R"] = f"vnm3_{qq}_{st}"
            for mc in range(8):
                def mm(e, mc=mc):
                    return e.matmul(pR.ap(), onescol_b[:], sq_bf.ap()[:, mc, :],
                                    start=(mc == 0), stop=(mc == 7))
                w = [f"sz3_{qq}_{st}_{mc}"] + (bw("R") if mc == 0 else [])
                op("tensor", mm, waits=w, tag=f"tsq3_{qq}_{st}" if mc == 7 else None)
            bank_w["R"] = f"tsq3_{qq}_{st}"
            op("vector", lambda e: e.tensor_scalar(brow_ms[:], pR.ap(), 1.0 / D, None,
                                                   ALU.mult),
               waits=[f"tsq3_{qq}_{st}"], tag=f"vms3_{qq}_{st}")
            bank_r["R"] = f"vms3_{qq}_{st}"
            op("vector", lambda e: e.tensor_tensor(brow_t[:], brow_nm[:], brow_nm[:],
                                                   ALU.mult))
            op("vector", lambda e: e.tensor_tensor(brow_t[:], brow_ms[:], brow_t[:],
                                                   ALU.subtract), tag=f"vv3_{qq}_{st}")
            op("scalar", lambda e: e.activation(brow_t[:], brow_t[:], AF.Ln,
                                                bias=cols.ap()[0:1, 197:198]),
               waits=[f"vv3_{qq}_{st}"])
            op("scalar", lambda e: e.activation(brow_ms[:], brow_t[:], AF.Exp, scale=-0.5),
               tag=f"ssd3_{qq}_{st}")
            op("vector", lambda e: e.tensor_tensor(brow_t[:], brow_nm[:], brow_ms[:],
                                                   ALU.mult),
               waits=[f"ssd3_{qq}_{st}"], tag=f"vr3_{qq}_{st}", force=True)
            for g in range(4):
                w = ([f"ssd3_{qq}_{st}"] + bw("X")) if g == 0 else []
                op("tensor", lambda e, g=g: e.transpose(
                    pX.ap()[:, g:g + 1], brow_ms.ap()[0:1, g * 128:(g + 1) * 128],
                    id_sb[0:1, 0:1]), waits=w)
            for g in range(4):
                w = [f"vr3_{qq}_{st}"] if g == 0 else []
                op("tensor", lambda e, g=g: e.transpose(
                    pX.ap()[:, 4 + g:5 + g], brow_t.ap()[0:1, g * 128:(g + 1) * 128],
                    id_sb[0:1, 0:1]), waits=w,
                    tag=f"trc_{qq}_{st}" if g == 3 else None)
            bank_w["X"] = f"trc_{qq}_{st}"
            wl3 = [f"trc_{qq}_{st}"]
            if qq > 0:
                wl3.append(f"s3_{qq-1}_{st}_3_1")
            op("vector", lambda e, st=st: e.tensor_copy(
                rsc.ap()[:, 172 + st * 8:180 + st * 8], pX.ap()[:, 0:8]),
               waits=wl3, tag=f"vrc_{qq}_{st}", force=True)
            bank_r["X"] = f"vrc_{qq}_{st}"
        def ln3_norm(st):
            tk = st * T
            for g in range(4):
                for half in range(2):
                    s = g * 2 + half
                    for j in range(4):
                        mc = half * 4 + j
                        pb3 = 4 + (s % 2)
                        def tp(e, mc=mc, g=g, tk=tk, j=j, pb3=pb3):
                            return e.transpose(p[pb3].ap()[:, j * 128:(j + 1) * 128],
                                               acc.ap()[:, mc, tk + g * 128:tk + (g + 1) * 128],
                                               id_sb[:])
                        w = bw(pb3) if j == 0 else []
                        op("tensor", tp, waits=w,
                           tag=f"tt3_{qq}_{st}_{g}_{half}" if j == 3 else None)
                    pb3 = 4 + (s % 2)
                    bank_w[pb3] = f"tt3_{qq}_{st}_{g}_{half}"
                    def ev3(e, g=g, half=half, pb3=pb3, st=st, s=s):
                        return e.activation(
                            out_tm.ap()[:, 0, (s % 2) * 512:(s % 2) * 512 + 512], p[pb3].ap(),
                            AF.Identity,
                            scale=rsc.ap()[:, 172 + st * 8 + g:173 + st * 8 + g],
                            bias=rsc.ap()[:, 176 + st * 8 + g:177 + st * 8 + g])
                    wl4 = [f"tt3_{qq}_{st}_{g}_{half}"]
                    if g == 0 and half == 0:
                        wl4.append(f"vrc_{qq}_{st}")
                    if dout_prev.get(s % 2):
                        wl4.append(dout_prev[s % 2])
                    op("scalar", ev3, waits=wl4, tag=f"s3_{qq}_{st}_{g}_{half}")
                    bank_r[pb3] = f"s3_{qq}_{st}_{g}_{half}"
                    tok = tok0 + tk + g * 128
                    op("sync", lambda e, tok=tok, half=half, s=s: e.dma_start(
                        out_hbm.ap()[tok:tok + 128, half * 512:(half + 1) * 512],
                        out_tm.ap()[:, 0, (s % 2) * 512:(s % 2) * 512 + 512]),
                        waits=[f"s3_{qq}_{st}_{g}_{half}"],
                        tag=f"dout_{qq}_{st}_{s}", dma=True, fam=f"dout{s % 2}")
                    dout_prev[s % 2] = f"dout_{qq}_{st}_{s}"
        pl = ([-1] + list(range(b_experts))) if b_experts > 0 else [-1]
        for pi, ei in enumerate(pl):
            ename = "s" if ei < 0 else str(ei)
            W1r = W1slots[gP % 2]
            first_pass = (pi == 0)
            for st in range(NST):
                hb = step % 2
                step += 1
                tk = st * T
                if pi == len(pl) - 1 and st == 1:
                    ln3_sq(0)
                if ei >= 0 and not first_pass:
                    wp = st
                    def mmw(e, wp=wp):
                        return e.matmul(p[4].ap(), ones1_b[:], wtmp[wp][0:1, :],
                                        start=True, stop=True)
                    op("tensor", mmw, waits=bw(4) + [f"vwsh_{qq}_{ei}_{st}"],
                       tag=f"twb_{qq}_{ei}_{st}")
                    bank_w[4] = f"twb_{qq}_{ei}_{st}"
                    h1_prev2[("wtmp", wp)] = f"twb_{qq}_{ei}_{st}"
                    op("scalar", lambda e: e.copy(wbc[:], p[4].ap()),
                       waits=[f"twb_{qq}_{ei}_{st}"], tag=f"swb_{qq}_{ei}_{st}")
                    bank_r[4] = f"swb_{qq}_{ei}_{st}"
                for mc in range(16):
                    b = pb_rot1[g1 % 2]
                    for kc in range(8):
                        def mm(e, b=b, kc=kc, mc=mc, tk=tk, W1r=W1r):
                            return e.matmul(p[b].ap(), W1r.ap()[:, kc, mc * 128:(mc + 1) * 128],
                                            hbB.ap()[:, kc, tk:tk + T],
                                            start=(kc == 0), stop=(kc == 7))
                        w = (bw(b) if kc == 0 else [])
                        if kc == 0 and mc == 0:
                            w = w + [f"dW1_{gP}", f"dhb{qq}_{st}"]
                        op("tensor", mm, waits=w,
                           tag=f"tmm1_{qq}_{ename}_{st}_{mc}" if kc == 7 else None)
                    bank_w[b] = f"tmm1_{qq}_{ename}_{st}_{mc}"
                    bias = (cols.ap()[:, C_SB1 + mc:C_SB1 + mc + 1] if ei < 0
                            else cols.ap()[:, C_RB1 + ei * 16 + mc:C_RB1 + ei * 16 + mc + 1])
                    wl2 = [f"tmm1_{qq}_{ename}_{st}_{mc}"]
                    if mc == 0 and hb in h1_prev2:
                        pv = h1_prev2[hb]
                        wl2 += pv if isinstance(pv, list) else [pv]
                    def ev(e, b=b, mc=mc, hb=hb, bias=bias):
                        return e.activation(h1[hb].ap()[:, mc, :], p[b].ap(), AF.Gelu, bias=bias)
                    op("scalar", ev, waits=wl2, tag=f"sg_{qq}_{ename}_{st}_{mc}")
                    bank_r[b] = f"sg_{qq}_{ename}_{st}_{mc}"
                    g1 += 1
                    if first_pass and mc == 6:
                        rt_p2[(qq, st)]()
                if ei >= 0:
                    for mc in range(16):
                        wl2 = [f"sg_{qq}_{ename}_{st}_{mc}"] +                             ([f"swb_{qq}_{ei}_{st}"] if mc == 0 else [])
                        op("vector", lambda e, mc=mc, hb=hb: e.tensor_tensor(
                            h1[hb].ap()[:, mc, :], h1[hb].ap()[:, mc, :], wbc[:], ALU.mult),
                            waits=wl2, tag=f"vsc_{qq}_{ei}_{st}" if mc == 15 else None)
                if pi == len(pl) - 1 and st == 1:
                    ln3_stats_p1(0, sq_done=True)
                nkc = 16
                for mc in range(8):
                    b = pb_rot2[g2 % 2]
                    for kc in range(nkc):
                        def mm2(e, b=b, kc=kc, mc=mc, hb=hb, ei=ei):
                            last = (kc == nkc - 1) and (ei != 0)
                            return e.matmul(p[b].ap(), W2.ap()[:, kc, mc * 128:(mc + 1) * 128],
                                            h1[hb].ap()[:, kc, :],
                                            start=(kc == 0), stop=last)
                        w = (bw(b) if kc == 0 else [])
                        if kc == 0 and mc == 0:
                            w = w + [f"dW2_{gP}"]
                            w = w + ([f"vsc_{qq}_{ei}_{st}"] if ei >= 0
                                     else [f"sg_{qq}_{ename}_{st}_15"])
                        op("tensor", mm2, waits=w,
                           tag=(f"tmm2_{qq}_{ename}_{st}_{mc}" if (kc == nkc - 1 and ei != 0) else None))
                    if ei == 0:
                        def mmr(e, b=b, mc=mc, tok0=tok0, tk=tk):
                            return e.matmul(p[b].ap(), rb2_sb.ap()[:, mc * 128:(mc + 1) * 128],
                                            w_hm.ap()[0:8, tok0 + tk:tok0 + tk + T],
                                            start=False, stop=True)
                        op("tensor", mmr, waits=([f"vw_{qq}_{st}"] if mc == 0 else []),
                           tag=f"tmm2_{qq}_{ename}_{st}_{mc}")
                    bank_w[b] = f"tmm2_{qq}_{ename}_{st}_{mc}"
                    wl2 = [f"tmm2_{qq}_{ename}_{st}_{mc}"] + \
                        ([f"dacc{qq}_{mc}"] if first_pass else [])
                    if ei < 0:
                        def ev2(e, b=b, mc=mc, tk=tk):
                            return e.scalar_tensor_tensor(
                                acc.ap()[:, mc, tk:tk + T], p[b].ap(),
                                cols.ap()[:, C_SB2 + mc:C_SB2 + mc + 1],
                                acc.ap()[:, mc, tk:tk + T], ALU.add, ALU.add)
                    else:
                        def ev2(e, b=b, mc=mc, tk=tk):
                            return e.tensor_tensor(acc.ap()[:, mc, tk:tk + T],
                                                   acc.ap()[:, mc, tk:tk + T], p[b].ap(), ALU.add)
                    op("vector", ev2, waits=wl2, tag=f"vac_{qq}_{ename}_{st}_{mc}")
                    bank_r[b] = f"vac_{qq}_{ename}_{st}_{mc}"
                    g2 += 1
                h1_prev2[hb] = f"tmm2_{qq}_{ename}_{st}_7"
                if pi == len(pl) - 1 and st == 1:
                    if qq + 1 < NQ:
                        emit_dhb(qq + 1)
                    ln3_stats_p1(1)
                    if qq + 1 < NQ:
                        emit_router(qq + 1)
            if gP + 2 < nP and passes[gP + 2][0] <= qq + 1:
                op("gpsimd", lambda e, g=gP: e.dma_start(
                    W1slots[g % 2].ap(), w1_src(passes[g + 2][1])),
                    waits=[f"tmm1_{qq}_{ename}_1_15"], tag=f"dW1_{gP+2}", dma=True, fam="dw1")
            if gP + 1 < nP:
                op("gpsimd", lambda e, g=gP: e.dma_start(W2.ap(), w2_src(passes[g + 1][1])),
                    waits=[f"tmm2_{qq}_{ename}_1_7"], tag=f"dW2_{gP+1}", dma=True, fam="dw2")
            if pi + 1 < len(pl) and pl[pi + 1] >= 0:
                nei = pl[pi + 1]
                for st2 in range(NST):
                    wp = st2
                    op("vector", lambda e, nei=nei, st2=st2, wp=wp, tok0=tok0: e.stream_shuffle(
                        wtmp[wp][:], w_hm.ap()[:, tok0 + st2 * T:tok0 + st2 * T + T],
                        [nei] * 32),
                        waits=[f"vw_{qq}_{st2}"] +
                              ([h1_prev2[("wtmp", wp)]] if ("wtmp", wp) in h1_prev2 else []),
                        tag=f"vwsh_{qq}_{nei}_{st2}")
            gP += 1
        for st in range(NST):
            ln3_norm(st)
        if qq + 1 < NQ:
            emit_dacc(qq + 1)
        h1_prev2[0] = f"tsq3_{qq}_{NST-1}"
        h1_prev2[1] = [f"dout_{qq}_{NST-1}_6", f"dout_{qq}_{NST-1}_7"]
        if qq == NQ - 1:
            dump("w", lambda: w_hm.ap(), [f"vw_{qq}_{NST-1}"])
    op("sync", None, waits=[f"dout_{NQ-1}_{NST-1}_6", f"dout_{NQ-1}_{NST-1}_7"])
    emit_all(nc, prog)
    return nc


_HMAT = np.zeros((128, 4, NH), dtype=ml_dtypes.bfloat16)
for _c in range(4):
    for _r in range(128):
        _HMAT[_r, _c, 2 * _c + _r // 64] = 0.125
_MCONST = np.zeros((NH, INNER), dtype=np.float32)
for _h in range(NH):
    _MCONST[_h, _h * 64:(_h + 1) * 64] = 1.0
_ID128 = np.eye(128, dtype=np.float32)


def kernel(**inputs):
    nc = build_nc()
    common = {
        "Wq": inputs["L0_attn_Wq"], "Wk": inputs["L0_attn_Wk"], "Wv": inputs["L0_attn_Wv"],
        "Wo": inputs["L0_attn_Wo"], "bq": inputs["L0_attn_bq"], "bk": inputs["L0_attn_bk"],
        "bv": inputs["L0_attn_bv"], "bo": inputs["L0_attn_bo"],
        "alpha": np.ascontiguousarray(inputs["L0_attn_alpha"].reshape(-1)),
        "augp": np.ascontiguousarray(inputs["L0_attn_aug_param"].reshape(-1)),
        "lns": inputs["L0_attn_ln_scale"], "lnb": inputs["L0_attn_ln_bias"],
        "rw": inputs["L0_ffn_router_w"], "rb": inputs["L0_ffn_router_b"],
        "sw1": inputs["L0_ffn_sw1"], "sb1": inputs["L0_ffn_sb1"],
        "sw2": inputs["L0_ffn_sw2"], "sb2": inputs["L0_ffn_sb2"],
        "rw1": inputs["L0_ffn_rw1"], "rb1": inputs["L0_ffn_rb1"],
        "rw2": inputs["L0_ffn_rw2"], "rb2": inputs["L0_ffn_rb2"],
        "id128": _ID128, "hmat": _HMAT, "mconst": _MCONST,
    }
    common = {k: np.ascontiguousarray(np.asarray(v)) for k, v in common.items()}
    x = np.asarray(inputs["L0_attn_x"])
    in_maps = [dict(common, x=np.ascontiguousarray(x[c])) for c in range(NCORES)]
    res = run_bass_kernel_spmd(nc, in_maps, core_ids=list(range(NCORES)))
    return np.stack([res.results[c]["out"] for c in range(NCORES)], axis=0)


# version 25
# speedup vs baseline: 1.0074x; 1.0074x over previous
import numpy as np
import ml_dtypes
import concourse.bass as bass
import concourse.mybir as mybir
from concourse.bass_utils import run_bass_kernel_spmd

F32 = mybir.dt.float32
BF16 = mybir.dt.bfloat16
U8 = mybir.dt.uint8
AF = mybir.ActivationFunctionType
ALU = mybir.AluOpType
AX = mybir.AxisListType

S, D, NH, HD = 4096, 1024, 8, 64
INNER, HID, NE = 512, 2048, 8
T, NT = 512, 8
QT, NQ, NST = 1024, 4, 2
NCORES = 8

SEMKEYS = ("tensor", "vector", "scalar", "gpsimd", "sync", "dsync", "dgp")


ENGS = ("tensor", "vector", "scalar", "gpsimd", "sync")


class Prog:

    def __init__(self, nc):
        self.nc = nc
        self.ops = []
        self.tags = {}
        self.cnt = {k: 0 for k in ENGS}
        self.sems = {k: nc.alloc_semaphore("sem_" + k) for k in ENGS}
        self.fam_last = {}
        self.barriers = []
        self._auto = 0

    def op(self, eng, fn, waits=(), tag=None, dma=False, fam=None, force=False):
        waits = list(waits)
        key = eng
        if dma:
            assert fam is not None, "dma op needs a family"
            key = "f_" + fam
            if key not in self.sems:
                self.sems[key] = self.nc.alloc_semaphore("sem_" + key)
                self.cnt[key] = 0
            if tag is None:
                tag = f"_auto{self._auto}"
                self._auto += 1
            if fam in self.fam_last:
                waits.append(self.fam_last[fam])
            self.fam_last[fam] = tag
        for w in waits:
            assert w in self.tags, f"wait on undefined tag {w}"
        if tag is not None:
            assert tag not in self.tags, f"duplicate tag {tag}"
            self.cnt[key] += 16 if dma else 1
            self.tags[tag] = (key, self.cnt[key])
        self.ops.append((eng, fn, tuple(waits), tag, key if tag is not None else None, force))

    def barrier(self):
        self.barriers.append(len(self.ops))


def _seg_emitter(nc, prog, seg, lastw, block):
    for engname in ENGS:
        ops_e = [o for o in seg if o[0] == engname]
        if not ops_e:
            continue

        def body(eng, ops_e=ops_e, engname=engname):
            lw = lastw[engname]
            for (_, fn, waits, tag, inc_key, force) in ops_e:
                for w in waits:
                    key, val = prog.tags[w]
                    if key == engname and engname != "gpsimd" and not force:
                        continue
                    if key != engname and not force and lw.get(key, 0) >= val:
                        continue
                    eng.wait_ge(prog.sems[key], val)
                    if key != engname:
                        lw[key] = max(lw.get(key, 0), val)
                instr = fn(eng) if fn is not None else None
                if tag is not None:
                    assert instr is not None
                    instr.then_inc(prog.sems[inc_key], 16 if inc_key.startswith("f_") else 1)
                    if not inc_key.startswith("f_"):
                        lw[inc_key] = max(lw.get(inc_key, 0), prog.tags[tag][1])

        getattr(block, engname)(body)


def emit_all(nc, prog):
    bounds = prog.barriers + [len(prog.ops)]
    cnt = {k: 0 for k in ENGS}
    marks = []
    bi = 0
    for idx, (eng, fn, waits, tag, inc_key, force) in enumerate(prog.ops):
        while bi < len(prog.barriers) and idx == prog.barriers[bi]:
            marks.append(dict(cnt))
            bi += 1
        if tag is not None and not inc_key.startswith("f_"):
            cnt[inc_key] += 1
    while bi < len(prog.barriers):
        marks.append(dict(cnt))
        bi += 1

    lastw = {e: {} for e in ENGS}
    start = 0
    for seg_i, end in enumerate(bounds):
        seg = prog.ops[start:end]
        if seg_i > 0:
            m = marks[seg_i - 1]
            for e in lastw:
                for k in ENGS:
                    lastw[e][k] = max(lastw[e].get(k, 0), m[k])
        if seg:
            with nc.Block() as block:
                _seg_emitter(nc, prog, seg, lastw, block)
        start = end


class Mem:
    def __init__(self, nc, base, cap):
        self.nc = nc
        self.base = base
        self.cap = cap
        self.off = 0

    def alloc(self, name, shape, dtype):
        nbytes = int(np.prod(shape[1:])) * (4 if dtype == F32 else 2)
        nbytes = (nbytes + 255) // 256 * 256
        h = self.nc.alloc_sbuf_tensor_at(name, list(shape), dtype,
                                         offset=self.base + self.off)
        self.off += nbytes
        assert self.off <= self.cap, f"SBUF overflow at {name}: {self.off} > {self.cap}"
        return h

    def mark(self):
        return self.off

    def reset(self, off):
        self.off = off


def build_nc(debug=(), stop_after="full", n_tiles=NT, a_stage="all", b_experts=NE, b_norm=True):
    nc = bass.Bass("TRN2", target_bir_lowering=False, detect_race_conditions=False)

    dI = lambda n, s: nc.dram_tensor(n, s, F32, kind="ExternalInput")
    x_hbm = dI("x", [S, D])
    Wq_h, Wk_h, Wv_h = dI("Wq", [D, INNER]), dI("Wk", [D, INNER]), dI("Wv", [D, INNER])
    Wo_h = dI("Wo", [INNER, D])
    bq_h, bk_h, bv_h = dI("bq", [INNER]), dI("bk", [INNER]), dI("bv", [INNER])
    bo_h = dI("bo", [D])
    alpha_h = dI("alpha", [NH * HD])
    augp_h = dI("augp", [NH * HD])
    lns_h, lnb_h = dI("lns", [D]), dI("lnb", [D])
    rw_h, rb_h = dI("rw", [D, NE]), dI("rb", [NE])
    sw1_h, sb1_h = dI("sw1", [D, HID]), dI("sb1", [HID])
    sw2_h, sb2_h = dI("sw2", [HID, D]), dI("sb2", [D])
    rw1_h, rb1_h = dI("rw1", [NE, D, HID]), dI("rb1", [NE, HID])
    rw2_h, rb2_h = dI("rw2", [NE, HID, D]), dI("rb2", [NE, D])
    id_h = dI("id128", [128, 128])
    hmat_h = nc.dram_tensor("hmat", [128, 4, NH], BF16, kind="ExternalInput")
    mconst_h = dI("mconst", [NH, INNER])
    out_hbm = nc.dram_tensor("out", [S, D], F32, kind="ExternalOutput")
    h32_hbm = nc.dram_tensor("h32scr", [8, 128, S], F32, kind="Internal")
    h16_hbm = nc.dram_tensor("h16scr", [8, 128, S], BF16, kind="Internal")
    dbg_t = {}
    for name, shape, dt in debug:
        dbg_t[name] = nc.dram_tensor("dbg_" + name, shape, dt, kind="ExternalOutput")

    remaining = nc.sbuf_bytes_remaining
    arena_bytes = remaining - 1024
    arena = nc.alloc_sbuf_tensor("arena", [128, arena_bytes], U8)
    base = nc.lookup_mloc(arena).addr
    mem = Mem(nc, base, arena_bytes)

    id_sb = mem.alloc("id_sb", [128, 128], F32)
    hmat = mem.alloc("hmat_sb", [128, 4, NH], BF16)
    onescol_f = mem.alloc("onescol_f", [128, 1], F32)
    onescol_b = mem.alloc("onescol_b", [128, 1], BF16)
    ones1_f = mem.alloc("ones1_f", [1, 128], F32)
    ones1_b = mem.alloc("ones1_b", [1, 128], BF16)
    ones8c = mem.alloc("ones8c", [8, 1], F32)
    w_hm = mem.alloc("w_hm", [32, S], BF16)
    rb2_sb = mem.alloc("rb2_sb", [8, D], BF16)
    cols = mem.alloc("cols", [128, 204], F32)
    C_DEC, C_AUG, C_BQ, C_BK, C_BV, C_BO = 0, 4, 8, 12, 16, 20
    C_LNS, C_LNB, C_SB2, C_SB1, C_RB1 = 28, 36, 44, 52, 68
    rb8_off = mem.mark()
    rb8col = mem.alloc("rb8col", [8, 1], F32)
    bosum = nc.alloc_sbuf_tensor_at("bosum", [1, 8], F32, offset=base + rb8_off + 32)
    wosum = nc.alloc_sbuf_tensor_at("wosum", [128, 4], BF16, offset=base + rb8_off + 64)
    rwsb = mem.alloc("rwsb", [128, 8, NE], BF16)
    union = mem.mark()

    x_tm = mem.alloc("x_tm", [128, 4, D], F32)
    rowstage = nc.alloc_sbuf_tensor_at("rowstage", [1, 2048], F32,
                                       offset=base + union)
    x_fm = [mem.alloc("x_fm0", [128, 8, T], F32), mem.alloc("x_fm1", [128, 8, T], F32)]
    x_bf = mem.alloc("x_bf", [128, 8, T], BF16)
    wq = mem.alloc("wq", [128, 8, INNER], BF16)
    wk = mem.alloc("wk", [128, 8, INNER], BF16)
    wv = mem.alloc("wv", [128, 8, INNER], BF16)
    wo = mem.alloc("wo", [128, 4, D], BF16)
    q_sb = [mem.alloc("q_sb0", [128, 4, T], BF16), mem.alloc("q_sb1", [128, 4, T], BF16)]
    k_sb = [mem.alloc("k_sb0", [128, 4, T], BF16), mem.alloc("k_sb1", [128, 4, T], BF16)]
    v_sb = [mem.alloc("v_sb0", [128, 4, T], BF16), mem.alloc("v_sb1", [128, 4, T], BF16)]
    sqk = mem.alloc("sqk", [128, 4, T], BF16)
    hs32 = mem.alloc("hs32", [32, T], F32)
    sm1 = mem.alloc("sm1", [32, T], F32)
    sm2 = mem.alloc("sm2", [32, T], F32)
    sm3 = mem.alloc("sm3", [32, T], F32)
    row_nm = mem.alloc("row_nm", [1, T], F32)
    row_ms = mem.alloc("row_ms", [1, T], F32)
    row_t = mem.alloc("row_t", [1, T], F32)
    row_rb = mem.alloc("row_rb", [1, T], BF16)
    row_mb = mem.alloc("row_mb", [1, T], BF16)
    arow_t = mem.alloc("arow_t", [1, T], F32)
    arow_rb = mem.alloc("arow_rb", [1, T], BF16)
    u_y = mem.alloc("u_y0", [128, 4, T], BF16)
    aug_bf = mem.alloc("aug_bf", [128, 1, T], BF16)
    itmp = nc.alloc_sbuf_tensor_at("itmp", [128, T], BF16,
                                   offset=nc.lookup_mloc(sm1).addr)
    itmp4 = nc.alloc_sbuf_tensor_at("itmp4", [128, 4, T], BF16,
                                    offset=nc.lookup_mloc(sm1).addr)
    inner_bf = mem.alloc("inner_bf", [128, 4, T], BF16)
    bufo = mem.alloc("bufo", [128, 8, T], F32)
    c1 = mem.alloc("c1", [128, 8, T], F32)
    o2_bf = mem.alloc("o2_bf", [128, 8, T], BF16)
    h_bf = mem.alloc("h_bf", [128, 8, T], BF16)
    B_sb = mem.alloc("B_sb", [8, INNER], BF16)
    attn_bf = nc.alloc_sbuf_tensor_at("attn_bf", [8, T], BF16,
                                      offset=nc.lookup_mloc(sm3).addr)
    a_end = mem.mark()

    mem.reset(union)
    acc = mem.alloc("acc", [128, 8, QT], F32)
    hbB = mem.alloc("hbB", [128, 8, QT], BF16)
    W1 = mem.alloc("W1", [128, 8, HID], BF16)
    W2 = mem.alloc("W2", [128, 16, D], BF16)
    h1a = mem.alloc("h1a", [128, 16, T], BF16)
    h1b = mem.alloc("h1b", [128, 16, T], BF16)
    h1 = [h1a, h1b]
    wbc = mem.alloc("wbc", [128, T], BF16)
    wtmp = [mem.alloc("wtmp0", [32, T], BF16), mem.alloc("wtmp1", [32, T], BF16)]
    out_tm = nc.alloc_sbuf_tensor_at("out_tm", [128, 4, D], F32,
                                     offset=nc.lookup_mloc(h1b).addr)
    WP = mem.alloc("WP", [128, 8, HID], BF16)
    sq_bf = nc.alloc_sbuf_tensor_at("sq_bf", [128, 8, T], BF16,
                                    offset=nc.lookup_mloc(h1a).addr)
    brow_nm = mem.alloc("brow_nm", [1, T], F32)
    brow_ms = mem.alloc("brow_ms", [1, T], F32)
    brow_t = mem.alloc("brow_t", [1, T], F32)
    brow_rb = mem.alloc("brow_rb", [1, T], BF16)
    brow_mb = mem.alloc("brow_mb", [1, T], BF16)
    rl8 = mem.alloc("rl8", [8, T], F32)
    rsc = mem.alloc("rsc", [128, 256], F32)
    b_end = mem.mark()
    import os as _os
    if _os.environ.get("KMEM_DEBUG"):
        print(f"[mem] arena={arena_bytes} persist={union} a_end={a_end} b_end={b_end} "
              f"slackA={arena_bytes-a_end} slackB={arena_bytes-b_end}")
    assert max(a_end, b_end) <= arena_bytes

    p = [nc.alloc_psum_tensor(f"p{i}", [128, 512], F32) for i in range(6)]
    pX = nc.alloc_psum_tensor("pX", [128, 512], F32)
    pR = nc.alloc_psum_tensor("pR", [1, 512], F32)

    prog = Prog(nc)
    bank_w = {}
    bank_r = {}

    def bw(b):
        return [bank_r[b]] if b in bank_r else []

    def br(b):
        return [bank_w[b]] if b in bank_w else []

    op = prog.op
    op("sync", lambda e: e.dma_start(id_sb[:], id_h.ap()), tag="d_id", dma=True, fam="dset")
    op("sync", lambda e: e.dma_start(hmat[:], hmat_h.ap()), tag="d_hm", dma=True, fam="dset")
    op("sync", lambda e: e.dma_start(sm2[0:8, :], mconst_h.ap()), tag="d_mc", dma=True, fam="dset")
    op("sync", lambda e: e.dma_start(rb8col[:], rb_h.ap().rearrange("(p o) -> p o", o=1)),
       tag="d_rb8", dma=True, fam="dset")
    op("gpsimd", lambda e: e.dma_start(wq[:], Wq_h.ap().rearrange("(kc p) m -> p kc m", p=128)),
       tag="d_wq", dma=True, fam="dwset")
    op("gpsimd", lambda e: e.dma_start(wk[:], Wk_h.ap().rearrange("(kc p) m -> p kc m", p=128)),
       tag="d_wk", dma=True, fam="dwset")
    op("gpsimd", lambda e: e.dma_start(wv[:], Wv_h.ap().rearrange("(kc p) m -> p kc m", p=128)),
       tag="d_wv", dma=True, fam="dwset")
    op("gpsimd", lambda e: e.dma_start(wo[:], Wo_h.ap().rearrange("(kc p) m -> p kc m", p=128)),
       tag="d_wo", dma=True, fam="dwset")
    op("gpsimd", lambda e: e.dma_start(rwsb[:], rw_h.ap().rearrange("(kc p) m -> p kc m", p=128)),
       tag="d_rw", dma=True, fam="dwset")
    op("gpsimd", lambda e: e.dma_start(rb2_sb[:], rb2_h.ap()), tag="d_rb2", dma=True, fam="dwset")
    op("vector", lambda e: e.memset(onescol_f[:], 1.0))
    op("vector", lambda e: e.memset(onescol_b[:], 1.0))
    op("vector", lambda e: e.memset(ones1_f[:], 1.0))
    op("vector", lambda e: e.memset(ones1_b[:], 1.0))
    op("vector", lambda e: e.memset(ones8c[:], 1.0))
    op("vector", lambda e: e.memset(hs32[:], -1e30))
    op("vector", lambda e: e.memset(sm3[:], -1e30), tag="v_ms3")
    op("vector", lambda e: e.memset(cols[0:1, 196:197], 1e-6))
    op("vector", lambda e: e.memset(cols[0:1, 197:198], 1e-5))
    op("vector", lambda e: e.memset(w_hm[:], 0.0))

    smalls = [
        ([(alpha_h, 0, 512), (augp_h, 512, 512), (bq_h, 1024, 512)],
         [(0, 512, C_DEC), (512, 512, C_AUG), (1024, 512, C_BQ)]),
        ([(bk_h, 0, 512), (bv_h, 512, 512), (bo_h, 1024, 1024)],
         [(0, 512, C_BK), (512, 512, C_BV), (1024, 1024, C_BO)]),
        ([(lns_h, 0, 1024), (lnb_h, 1024, 1024)],
         [(0, 1024, C_LNS), (1024, 1024, C_LNB)]),
        ([(sb2_h, 0, 1024)], [(0, 1024, C_SB2)]),
        ([(sb1_h, 0, 2048)], [(0, 2048, C_SB1)]),
    ] + [
        ([(None, 0, 2048)], [(0, 2048, C_RB1 + e * 16)]) for e in range(NE)
    ]
    prev_evac = None
    setup_tidx = 0
    for bi_, (dmas, moves) in enumerate(smalls):
        dtag = f"sd{bi_}"
        for j, (src, roff, ln) in enumerate(dmas):
            ap = (rb1_h.ap()[bi_ - 5] if src is None else src.ap()).flatten()
            wlist = [prev_evac] if (j == 0 and prev_evac) else []
            if bi_ == 1 and j == 0:
                wlist.append("s_sigr")
            op("sync",
               lambda e, ap=ap, roff=roff, ln=ln: e.dma_start(
                   rowstage[:, roff:roff + ln], ap.rearrange("(o f) -> o f", o=1)),
               waits=wlist, tag=f"{dtag}_{j}", dma=True, fam="drow")
        dma_tags = [f"{dtag}_{j}" for j in range(len(dmas))]
        ncols = sum(ln for _, ln, _ in moves) // 128
        colptr = 0
        for roff, ln, coff in moves:
            for jj in range(ln // 128):
                def tp(e, roff=roff, jj=jj, colptr=colptr):
                    return e.transpose(p[0].ap()[:, colptr:colptr + 1],
                                       rowstage[0:1, roff + jj * 128:roff + (jj + 1) * 128],
                                       id_sb[0:1, 0:1])
                w = dma_tags + ["d_id"] + (bw(0) if (colptr == 0) else [])
                op("tensor", tp, waits=w,
                   tag=f"st{setup_tidx}" if (jj == ln // 128 - 1 and (roff, ln, coff) == moves[-1]) else None)
                colptr += 1
        bank_w[0] = f"st{setup_tidx}"
        colptr = 0
        last = None
        for mi, (roff, ln, coff) in enumerate(moves):
            n = ln // 128
            def ev(e, colptr=colptr, n=n, coff=coff):
                return e.tensor_copy(cols[:, coff:coff + n], p[0].ap()[:, colptr:colptr + n])
            last = f"se{setup_tidx}_{mi}"
            op("vector", ev, waits=[f"st{setup_tidx}"], tag=last)
            colptr += n
        bank_r[0] = last
        prev_evac = last
        setup_tidx += 1
        if bi_ == 0:
            op("scalar", lambda e: e.activation(sm1[0:1, 0:512], rowstage[0:1, 0:512],
                                                AF.Sigmoid),
               waits=["sd0_0", "v_ms3"], tag="s_sigr")
            op("vector", lambda e: e.tensor_scalar(sm1[0:1, 0:512], sm1[0:1, 0:512], -1.0, 1.0,
                                                   ALU.mult, ALU.add), waits=["s_sigr"],
               tag="v_omd")
            op("tensor", lambda e: e.matmul(p[2][:], ones1_f[:], sm1[0:1, 0:512],
                                            start=True, stop=True),
               waits=["v_omd"] + bw(2), tag="t_omb")
            bank_w[2] = "t_omb"
            op("vector", lambda e: e.tensor_tensor(B_sb[:], sm2[0:8, :], p[2].ap()[0:8, :],
                                                   ALU.mult),
               waits=["t_omb", "d_mc"], tag="v_bsb")
            bank_r[2] = "v_bsb"
            op("scalar", lambda e: e.activation(cols[:, C_DEC:C_DEC + 4],
                                                cols[:, C_DEC:C_DEC + 4], AF.Sigmoid),
               waits=["se0_0"], tag="s_dec")

    op("vector", lambda e: e.tensor_copy(o2_bf.ap()[:, 0, 0:128], id_sb[:]),
       waits=["d_id"], tag="s_idb")
    for kc in range(4):
        for mb in range(8):
            def tpw(e, kc=kc, mb=mb):
                return e.transpose(
                    p[3].ap().bitcast(BF16)[:, (mb % 4) * 128:(mb % 4) * 128 + 128],
                    wo.ap()[:, kc, mb * 128:(mb + 1) * 128],
                    o2_bf.ap()[:, 0, 0:128])
            w = (["d_wo", "s_idb"] if (kc == 0 and mb == 0) else []) + \
                (bw(3) if mb % 4 == 0 else [])
            op("tensor", tpw, waits=w, tag=f"s_wt{kc}_{mb}" if mb % 4 == 3 else None)
            if mb % 4 == 3:
                bank_w[3] = f"s_wt{kc}_{mb}"
                op("vector", lambda e, kc=kc, mb=mb: e.tensor_copy(
                    u_y.ap()[:, mb // 4, :], p[3].ap().bitcast(BF16)[:, 0:512]),
                   waits=[f"s_wt{kc}_{mb}"], tag=f"s_we{kc}_{mb}")
                bank_r[3] = f"s_we{kc}_{mb}"
        for mb in range(8):
            def msum(e, kc=kc, mb=mb):
                return e.matmul(p[3].ap()[:, 0:1],
                                u_y.ap()[:, mb // 4, (mb % 4) * 128:(mb % 4) * 128 + 128],
                                onescol_b[:], start=(mb == 0), stop=(mb == 7))
            w = ([f"s_we{kc}_7"] + bw(3)) if mb == 0 else []
            op("tensor", msum, waits=w, tag=f"s_ws{kc}" if mb == 7 else None)
        bank_w[3] = f"s_ws{kc}"
        op("vector", lambda e, kc=kc: e.tensor_copy(wosum.ap()[:, kc:kc + 1],
                                                    p[3].ap()[:, 0:1]),
           waits=[f"s_ws{kc}"], tag=f"s_wc{kc}")
        bank_r[3] = f"s_wc{kc}"
    op("tensor", lambda e: e.matmul(pR.ap()[0:1, 0:8], onescol_f[:],
                                    cols.ap()[:, C_BO:C_BO + 8], start=True, stop=True),
       waits=["se1_2"] + bw("R"), tag="s_bo1")
    bank_w["R"] = "s_bo1"
    op("vector", lambda e: e.tensor_copy(bosum[:], pR.ap()[0:1, 0:8]),
       waits=["s_bo1"], tag="s_bo2")
    bank_r["R"] = "s_bo2"
    op("vector", lambda e: e.tensor_tensor(bosum.ap()[:, 0:4], bosum.ap()[:, 0:4],
                                           bosum.ap()[:, 4:8], ALU.add),
       waits=["s_bo2"], tag="s_bo3", force=True)
    op("vector", lambda e: e.tensor_tensor(bosum.ap()[:, 0:2], bosum.ap()[:, 0:2],
                                           bosum.ap()[:, 2:4], ALU.add),
       waits=["s_bo3"], tag="s_bo4", force=True)
    op("vector", lambda e: e.tensor_tensor(bosum.ap()[:, 0:1], bosum.ap()[:, 0:1],
                                           bosum.ap()[:, 1:2], ALU.add),
       waits=["s_bo4"], tag="s_bo5", force=True)

    prog.barrier()

    if stop_after == "setup":
        op("sync", None, waits=["d_rb2", "d_id"])
        emit_all(nc, prog)
        return nc

    dec_ap = lambda c: cols.ap()[:, C_DEC + c:C_DEC + c + 1]

    def dump(name, apf, wtags, stall_eng="vector"):
        if name in dbg_t:
            op("sync", lambda e, apf=apf, name=name: e.dma_start(dbg_t[name].ap(), apf()),
               waits=wtags, tag="dbg_" + name, dma=True, fam="dbg_" + name)
            op(stall_eng, None, waits=["dbg_" + name])

    rot = [0, 1]
    rotu = [4, 5]
    state = {"gbig": 0, "gu": 0}

    def emit_front(i):
        pf = i % 2
        wl = [f"ttx{i-1}_7"] if i > 0 else []
        op("sync", lambda e, i=i: e.dma_start(
            x_tm[:], x_hbm.ap()[i * T:(i + 1) * T, :].rearrange("(g p) d -> p g d", p=128)),
            waits=wl, tag=f"dx{i}", dma=True, fam="dx")
        for c in range(8):
            b = rot[state["gbig"] % 2]
            for g in range(4):
                def tp(e, c=c, g=g, b=b, i=i):
                    return e.transpose(p[b].ap()[:, g * 128:(g + 1) * 128],
                                       x_tm.ap()[:, g, c * 128:(c + 1) * 128], id_sb[:])
                w = ([f"dx{i}"] if (c == 0 and g == 0) else []) + (bw(b) if g == 0 else [])
                op("tensor", tp, waits=w, tag=f"ttx{i}_{c}" if g == 3 else None)
            bank_w[b] = f"ttx{i}_{c}"
            wl2 = [f"ttx{i}_{c}"] + ([f"vz{i-2}"] if (i > 1 and c == 0) else [])
            op("scalar", lambda e, c=c, b=b, pf=pf: e.copy(x_fm[pf].ap()[:, c, :], p[b].ap()),
               waits=wl2, tag=f"sxf{i}_{c}")
            wl2b = [f"tq{i-1}_v_3"] if (i > 0 and c == 0) else []
            op("scalar", lambda e, c=c, b=b: e.copy(x_bf.ap()[:, c, :], p[b].ap()),
               waits=wl2b, tag=f"gxb{i}_{c}")
            bank_r[b] = f"gxb{i}_{c}"
            state["gbig"] += 1
        for wname, wsb, dstl, bcol in (
                ("q", wq, q_sb, C_BQ), ("k", wk, k_sb, C_BK), ("v", wv, v_sb, C_BV)):
            for mc in range(4):
                b = rot[state["gbig"] % 2]
                for kc in range(8):
                    def mm(e, b=b, kc=kc, mc=mc, wsb=wsb):
                        return e.matmul(p[b].ap(), wsb.ap()[:, kc, mc * 128:(mc + 1) * 128],
                                        x_bf.ap()[:, kc, :], start=(kc == 0), stop=(kc == 7))
                    w = bw(b) if kc == 0 else []
                    if kc == 0 and mc == 0 and wname == "q":
                        w = w + [f"gxb{i}_{c}" for c in range(8)] + (["d_rb2"] if i == 0 else [])
                    op("tensor", mm, waits=w, tag=f"tq{i}_{wname}_{mc}" if kc == 7 else None)
                bank_w[b] = f"tq{i}_{wname}_{mc}"
                wl2 = [f"tq{i}_{wname}_{mc}"]
                if i > 1 and mc == 0:
                    wl2 += {"q": [f"vin{i-2}_3"], "k": [f"vsq{i-2}_3"], "v": [f"vu{i-2}_3"]}[wname]
                def ev(e, b=b, mc=mc, dstl=dstl, bcol=bcol, pf=pf):
                    return e.activation(dstl[pf].ap()[:, mc, :], p[b].ap(), AF.Identity,
                                        bias=cols.ap()[:, bcol + mc:bcol + mc + 1])
                op("scalar", ev, waits=wl2, tag=f"sq{i}_{wname}_{mc}")
                bank_r[b] = f"sq{i}_{wname}_{mc}"
                state["gbig"] += 1

    def butterfly(srcs, dst, tmp, opk, tagp, src_tag):
        prev = src_tag
        for r, x in enumerate((4, 2, 1)):
            mask = [(j ^ x) % 32 for j in range(32)]
            sa = srcs if r == 0 else dst
            t1 = f"{tagp}_s{r}"
            op("vector", lambda e, sa=sa, tmp=tmp, mask=mask: e.stream_shuffle(
                tmp[:], sa[:], mask), waits=[prev], tag=t1, force=True)
            t2 = tagp if r == 2 else f"{tagp}_m{r}"
            op("vector", lambda e, sa=sa, dst=dst, tmp=tmp: e.tensor_tensor(
                dst[:], sa[:], tmp[:], opk), waits=[t1], tag=t2, force=True)
            prev = t2

    def emit_attn(i):
        pf = i % 2
        ti = i % 2
        if a_stage == "qkv":
            return
        for c in range(4):
            wl2 = [f"sq{i}_q_{c}", f"sq{i}_k_{c}"]
            op("vector", lambda e, c=c, pf=pf: e.tensor_tensor(
                sqk.ap()[:, c, :], q_sb[pf].ap()[:, c, :], k_sb[pf].ap()[:, c, :], ALU.mult),
                waits=wl2, tag=f"vsq{i}_{c}")
        for c in range(4):
            def mm(e, c=c):
                return e.matmul(pX.ap()[0:8, :], hmat.ap()[:, c, :], sqk.ap()[:, c, :],
                                start=(c == 0), stop=(c == 3))
            w = [f"vsq{i}_{c}"] + (["d_hm"] if i == 0 and c == 0 else []) + (bw("X") if c == 0 else [])
            op("tensor", mm, waits=w, tag=f"ths{i}" if c == 3 else None)
        bank_w["X"] = f"ths{i}"
        op("vector", lambda e: e.tensor_copy(hs32[0:8, :], pX.ap()[0:8, :]),
           waits=[f"ths{i}"], tag=f"vhs{i}")
        bank_r["X"] = f"vhs{i}"
        butterfly(hs32, sm1, sm2, ALU.max, f"vbm{i}", f"vhs{i}")
        op("vector", lambda e: e.tensor_tensor(sm2[0:8, :], hs32[0:8, :], sm1[0:8, :],
                                               ALU.subtract), tag=f"vsub{i}")
        op("scalar", lambda e: e.activation(sm2[0:8, :], sm2[0:8, :], AF.Exp),
           waits=[f"vsub{i}"], tag=f"sexp{i}")
        op("tensor", lambda e: e.matmul(pX.ap()[0:1, :], ones8c[:], sm2[0:8, :],
                                        start=True, stop=True),
           waits=[f"sexp{i}"] + bw("X"), tag=f"tz{i}")
        bank_w["X"] = f"tz{i}"
        op("scalar", lambda e: e.activation(arow_t[:], pX.ap()[0:1, :], AF.Ln), waits=[f"tz{i}"])
        op("scalar", lambda e: e.activation(arow_rb[:], arow_t[:], AF.Exp, scale=-1.0),
           tag=f"vrz{i}")
        bank_r["X"] = f"vrz{i}"
        op("tensor", lambda e: e.matmul(pX.ap(), ones1_b[:], arow_rb[:], start=True, stop=True),
           waits=[f"vrz{i}"] + bw("X"), tag=f"tzb{i}")
        bank_w["X"] = f"tzb{i}"
        op("vector", lambda e: e.tensor_tensor(attn_bf[:], sm2[0:8, :], pX.ap()[0:8, :],
                                               ALU.mult), waits=[f"tzb{i}"], tag=f"vat{i}")
        bank_r["X"] = f"vat{i}"
        for c in range(4):
            b = rotu[state["gu"] % 2]
            def mm(e, c=c, b=b):
                return e.matmul(p[b].ap(), B_sb.ap()[:, c * 128:(c + 1) * 128], attn_bf[:],
                                start=True, stop=True)
            w = [f"vat{i}"] + bw(b) + (["v_bsb"] if i == 0 and c == 0 else [])
            op("tensor", mm, waits=w, tag=f"tu{i}_{c}")
            bank_w[b] = f"tu{i}_{c}"
            wl2 = [f"tu{i}_{c}", f"sq{i}_v_{c}"]
            op("vector", lambda e, c=c, b=b: e.tensor_tensor(
                u_y.ap()[:, c, :], p[b].ap(), v_sb[pf].ap()[:, c, :], ALU.mult),
                waits=wl2, tag=f"vu{i}_{c}")
            bank_r[b] = f"vu{i}_{c}"
            state["gu"] += 1
            init = 0.0 if i == 0 else cols.ap()[:, 198 + c:199 + c]
            op("vector", lambda e, c=c, init=init: e.tensor_tensor_scan(
                u_y.ap()[:, c, :], dec_ap(c).broadcast_to([128, T]),
                u_y.ap()[:, c, :], init, ALU.mult, ALU.add),
                waits=(["s_dec"] if i == 0 and c == 0 else []), tag=None)
            op("vector", lambda e, c=c: e.tensor_copy(
                cols.ap()[:, 198 + c:199 + c], u_y.ap()[:, c, T - 1:T]),
                tag=f"vsc{i}_{c}")
        if a_stage == "scan":
            return
    def emit_wl1(i):
        pf = i % 2
        ti = i % 2
        if a_stage in ("qkv", "scan"):
            return
        for c in range(4):
            wl2 = [f"sq{i}_q_{c}"]
            if c == 0 and i > 0:
                wl2 += [f"vin{i-1}_3", f"tsq2_{i-1}"]
            op("scalar", lambda e, c=c, pf=pf: e.activation(
                o2_bf.ap()[:, 4 + c, :], q_sb[pf].ap()[:, c, :], AF.Sigmoid,
                scale=cols.ap()[:, C_AUG + c:C_AUG + c + 1]), waits=wl2, tag=f"saug{i}_{c}")
        wl2 = [f"vsc{i}_3", f"saug{i}_3"] + ([f"two{i-1}_7"] if i > 0 else [])
        op("vector", lambda e, pf=pf: e.tensor_tensor(
            itmp4.ap(), q_sb[pf].ap(), u_y.ap(), ALU.mult), waits=wl2, tag=f"vit{i}",
            force=True)
        op("vector", lambda e: e.tensor_tensor(
            inner_bf.ap(), itmp4.ap(), o2_bf.ap()[:, 4:8, :], ALU.add),
            tag=f"vin{i}_3", force=True)
        for kc in range(4):
            def mm(e, kc=kc):
                return e.matmul(pR.ap(), wosum.ap()[:, kc:kc + 1], inner_bf.ap()[:, kc, :],
                                start=(kc == 0), stop=(kc == 3))
            w = (bw("R") + [f"vin{i}_3"]) if kc == 0 else []
            op("tensor", mm, waits=w, tag=f"tm1_{i}" if kc == 3 else None)
        bank_w["R"] = f"tm1_{i}"
        op("vector", lambda e: e.tensor_scalar(row_nm[:], pR.ap(), bosum.ap()[0:1, 0:1],
                                               -1.0 / D, ALU.add, ALU.mult),
           waits=[f"tm1_{i}"], tag=f"vnm{i}")
        bank_r["R"] = f"vnm{i}"
        for mc in range(8):
            b = rotu[state["gu"] % 2]
            for kc in range(4):
                def mm(e, b=b, kc=kc, mc=mc):
                    return e.matmul(p[b].ap(), wo.ap()[:, kc, mc * 128:(mc + 1) * 128],
                                    inner_bf.ap()[:, kc, :], start=(kc == 0), stop=(kc == 3))
                w = (bw(b) if kc == 0 else []) + ([f"vin{i}_3"] if (kc == 0 and mc == 0) else [])
                op("tensor", mm, waits=w, tag=f"two{i}_{mc}" if kc == 3 else None)
            bank_w[b] = f"two{i}_{mc}"
            wl2 = [f"two{i}_{mc}"] + ([f"dh32_{i-1}", f"gh{i-1}"] if (i > 0 and mc == 0) else [])
            op("scalar", lambda e, b=b, mc=mc: e.activation(
                bufo.ap()[:, mc, :], p[b].ap(), AF.Identity,
                bias=cols.ap()[:, C_BO + mc:C_BO + mc + 1]), waits=wl2, tag=f"so{i}_{mc}")
            bank_r[b] = f"so{i}_{mc}"
            state["gu"] += 1
        if i == 0:
            dump("o", lambda: bufo.ap(), [f"so{i}_{mc}" for mc in range(8)], stall_eng="scalar")
        if a_stage == "wo":
            return
        for mc in range(8):
            wl2 = [f"so{i}_{mc}"] + ([f"tsq1_{i-1}"] if (i > 0 and mc == 0) else [])
            op("scalar", lambda e, mc=mc: e.activation(
                o2_bf.ap()[:, mc, :], bufo.ap()[:, mc, :], AF.Square), waits=wl2,
                tag=f"so2{i}_{mc}")
        for mc in range(8):
            def mm(e, mc=mc):
                return e.matmul(pR.ap(), onescol_b[:], o2_bf.ap()[:, mc, :],
                                start=(mc == 0), stop=(mc == 7))
            w = [f"so2{i}_{mc}"] + (bw("R") if mc == 0 else [])
            op("tensor", mm, waits=w, tag=f"tsq1_{i}" if mc == 7 else None)
        bank_w["R"] = f"tsq1_{i}"
        op("vector", lambda e: e.tensor_scalar(row_ms[:], pR.ap(), 1.0 / D, None, ALU.mult),
           waits=[f"tsq1_{i}"], tag=f"vms{i}")
        bank_r["R"] = f"vms{i}"
        op("vector", lambda e: e.tensor_tensor(row_t[:], row_nm[:], row_nm[:], ALU.mult))
        op("vector", lambda e: e.tensor_tensor(row_t[:], row_ms[:], row_t[:], ALU.subtract),
           tag=f"vvar{i}")
        op("scalar", lambda e: e.activation(row_t[:], row_t[:], AF.Ln, bias=cols.ap()[0:1, 196:197]),
           waits=[f"vvar{i}"])
        op("scalar", lambda e: e.activation(row_rb[:], row_t[:], AF.Exp, scale=-0.5),
           tag=f"ssd{i}")
        op("vector", lambda e: e.tensor_tensor(row_mb[:], row_nm[:], row_rb[:], ALU.mult),
           waits=[f"ssd{i}"], tag=f"vr{i}")
        op("tensor", lambda e: e.matmul(p[2].ap(), ones1_b[:], row_rb[:], start=True, stop=True),
           waits=[f"vr{i}"] + bw(2), tag=None)
        op("tensor", lambda e: e.matmul(p[3].ap(), ones1_b[:], row_mb[:], start=True, stop=True),
           waits=bw(3), tag=f"tbc1_{i}")
        bank_w[2] = f"tbc1_{i}"
        bank_w[3] = f"tbc1_{i}"
        for mc in range(8):
            wl2 = ([f"tbc1_{i}"] if mc == 0 else []) + ([f"sz2_{i-1}_7"] if (i > 0 and mc == 0) else [])
            op("vector", lambda e, mc=mc: e.tensor_tensor(
                c1.ap()[:, mc, :], bufo.ap()[:, mc, :], p[2].ap(), ALU.mult), waits=wl2)
            op("vector", lambda e, mc=mc: e.tensor_tensor(
                bufo.ap()[:, mc, :], c1.ap()[:, mc, :], p[3].ap(), ALU.add),
                tag=f"vln1_{i}" if mc == 7 else None)
        for mc in range(8):
            op("vector", lambda e, mc=mc: e.tensor_scalar(
                c1.ap()[:, mc, :], bufo.ap()[:, mc, :],
                cols.ap()[:, C_LNS + mc:C_LNS + mc + 1],
                cols.ap()[:, C_LNB + mc:C_LNB + mc + 1], ALU.mult, ALU.add))
            op("vector", lambda e, mc=mc, pf=pf: e.tensor_tensor(
                c1.ap()[:, mc, :], c1.ap()[:, mc, :], x_fm[pf].ap()[:, mc, :], ALU.add),
                tag=f"vz{i}" if mc == 7 else None)
    def emit_back2(i):
        pf = i % 2
        ti = i % 2
        if a_stage in ("qkv", "scan", "wo"):
            return
        for mc in range(8):
            wl2 = [f"vz{i}"] + ([f"tsq1_{i}"] if mc == 0 else [])
            op("scalar", lambda e, mc=mc: e.activation(
                o2_bf.ap()[:, mc, :], c1.ap()[:, mc, :], AF.Square), waits=wl2,
                tag=f"sz2_{i}_{mc}")
        for mc in range(8):
            def mm(e, mc=mc):
                return e.matmul(pR.ap(), onescol_f[:], c1.ap()[:, mc, :],
                                start=(mc == 0), stop=(mc == 7))
            w = ([f"vz{i}"] + bw("R")) if mc == 0 else []
            op("tensor", mm, waits=w, tag=f"tm2_{i}" if mc == 7 else None)
        bank_w["R"] = f"tm2_{i}"
        op("vector", lambda e: e.tensor_scalar(row_nm[:], pR.ap(), -1.0 / D, None, ALU.mult),
           waits=[f"tm2_{i}"], tag=f"vnm2{i}")
        bank_r["R"] = f"vnm2{i}"
        for mc in range(8):
            def mm(e, mc=mc):
                return e.matmul(pR.ap(), onescol_b[:], o2_bf.ap()[:, mc, :],
                                start=(mc == 0), stop=(mc == 7))
            w = [f"sz2_{i}_{mc}"] + (bw("R") if mc == 0 else [])
            op("tensor", mm, waits=w, tag=f"tsq2_{i}" if mc == 7 else None)
        bank_w["R"] = f"tsq2_{i}"
        op("vector", lambda e: e.tensor_scalar(row_ms[:], pR.ap(), 1.0 / D, None, ALU.mult),
           waits=[f"tsq2_{i}"], tag=f"vms2{i}")
        bank_r["R"] = f"vms2{i}"
        op("vector", lambda e: e.tensor_tensor(row_t[:], row_nm[:], row_nm[:], ALU.mult))
        op("vector", lambda e: e.tensor_tensor(row_t[:], row_ms[:], row_t[:], ALU.subtract),
           tag=f"vvar2{i}")
        op("scalar", lambda e: e.activation(row_t[:], row_t[:], AF.Ln, bias=cols.ap()[0:1, 197:198]),
           waits=[f"vvar2{i}"])
        op("scalar", lambda e: e.activation(row_rb[:], row_t[:], AF.Exp, scale=-0.5),
           tag=f"ssd2{i}")
        op("vector", lambda e: e.tensor_tensor(row_mb[:], row_nm[:], row_rb[:], ALU.mult),
           waits=[f"ssd2{i}"], tag=f"vr2{i}")
        wl2 = [f"vr2{i}", f"vln1_{i}"]
        op("tensor", lambda e: e.matmul(p[2].ap(), ones1_b[:], row_rb[:], start=True, stop=True),
           waits=wl2, tag=None)
        op("tensor", lambda e: e.matmul(p[3].ap(), ones1_b[:], row_mb[:], start=True, stop=True),
           waits=[], tag=f"tbc2_{i}")
        bank_w[2] = f"tbc2_{i}"
        bank_w[3] = f"tbc2_{i}"
        for mc in range(8):
            wl2 = [f"tbc2_{i}"] if mc == 0 else []
            op("vector", lambda e, mc=mc: e.tensor_tensor(
                bufo.ap()[:, mc, :], c1.ap()[:, mc, :], p[2].ap(), ALU.mult), waits=wl2)
            op("vector", lambda e, mc=mc: e.tensor_tensor(
                bufo.ap()[:, mc, :], bufo.ap()[:, mc, :], p[3].ap(), ALU.add),
                tag=f"vh{i}" if mc == 7 else None)
        bank_r[2] = f"vh{i}"
        bank_r[3] = f"vh{i}"
        for mc in range(8):
            wl2 = [f"vh{i}"] + ([f"dh16_{i-1}"] if (i > 0 and mc == 0) else [])
            op("scalar", lambda e, mc=mc: e.copy(h_bf.ap()[:, mc, :], bufo.ap()[:, mc, :]),
               waits=wl2, tag=f"gh{i}" if mc == 7 else None)
        op("sync", lambda e, i=i: e.dma_start(
            h32_hbm.ap()[:, :, i * T:(i + 1) * T].rearrange("c p t -> p c t"), bufo.ap()),
            waits=[f"vh{i}"], tag=f"dh32_{i}", dma=True, fam="dh32")
        op("sync", lambda e, i=i: e.dma_start(
            h16_hbm.ap()[:, :, i * T:(i + 1) * T].rearrange("c p t -> p c t"), h_bf.ap()),
            waits=[f"gh{i}"], tag=f"dh16_{i}", dma=True, fam="dh16")
        if i == 0:
            dump("h", lambda: bufo.ap(), [f"vh{i}"], stall_eng="scalar")
            dump("q", lambda: q_sb[0].ap(), [f"sq{i}_q_3"], stall_eng="scalar")
            dump("attn", lambda: attn_bf.ap(), [f"vat{i}"])
            dump("st", lambda: u_y.ap(), [f"vsc{i}_3"])

    emit_front(0)
    emit_attn(0)
    emit_wl1(0)
    for i in range(n_tiles):
        if i + 1 < n_tiles:
            emit_front(i + 1)
        emit_back2(i)
        if i + 1 < n_tiles:
            emit_attn(i + 1)
            emit_wl1(i + 1)

    W1slots = [W1, WP]
    passes = []
    for _q in range(NQ):
        _pl = ([-1] + list(range(b_experts))) if b_experts > 0 else [-1]
        passes += [(_q, _e) for _e in _pl]
    nP = len(passes)

    def w1_src(ei):
        return (sw1_h.ap() if ei < 0 else rw1_h.ap()[ei]).rearrange("(kc p) m -> p kc m", p=128)

    def w2_src(ei):
        return (sw2_h.ap() if ei < 0 else rw2_h.ap()[ei]).rearrange("(kc p) m -> p kc m", p=128)

    op("gpsimd", lambda e: e.dma_start(W1slots[0].ap(), w1_src(passes[0][1])),
       waits=[f"sq{n_tiles-1}_v_3"], tag="dW1_0", dma=True, fam="dw1")
    op("gpsimd", lambda e: e.dma_start(W2.ap(), w2_src(passes[0][1])),
       waits=[f"two{n_tiles-1}_7"], tag="dW2_0", dma=True, fam="dw2")

    prog.barrier()

    if stop_after == "A":
        lt = n_tiles - 1
        wmap = {"qkv": [f"sq{lt}_v_3"], "scan": [f"vsc{lt}_3"], "wo": [f"so{lt}_7"],
                "ln": [f"vh{lt}"], "all": [f"dh32_{lt}", f"dh16_{lt}"]}
        for eng in ("sync",):
            op(eng, None, waits=wmap[a_stage])
        emit_all(nc, prog)
        return nc

    h1_prev2 = {}
    pb_rot1 = [0, 1]
    pb_rot2 = [2, 3]
    g1 = 0
    g2 = 0
    step = 0

    op("gpsimd", lambda e: e.dma_start(W1slots[1].ap(), w1_src(passes[1][1])),
       waits=[f"dh32_{NT-1}"], tag="dW1_1", dma=True, fam="dw1")

    laste_g = str(b_experts - 1) if b_experts > 0 else "s"

    def emit_dhb(q2):
        t0 = q2 * QT
        for st in range(NST):
            wl = [f"tmm1_{q2-1}_{laste_g}_{st}_15"] if q2 > 0 else [f"dh16_{NT-1}"]
            op("sync", lambda e, t0=t0, st=st: e.dma_start(
                hbB.ap()[:, :, st * T:(st + 1) * T],
                h16_hbm.ap()[:, :, t0 + st * T:t0 + (st + 1) * T].rearrange("c p t -> p c t")),
                waits=wl, tag=f"dhb{q2}_{st}", dma=True, fam="dhb")

    def emit_dacc(q2):
        t0 = q2 * QT
        for mc in range(8):
            wl = [f"tt3_{q2-1}_1_3_{mc // 4}"] if q2 > 0 else [f"dh32_{NT-1}"]
            op("sync", lambda e, t0=t0, mc=mc: e.dma_start(
                acc.ap()[:, mc, :], h32_hbm.ap()[mc, :, t0:t0 + QT]),
                waits=wl, tag=f"dacc{q2}_{mc}", dma=True, fam="dacc")

    rt_p2 = {}

    def emit_router(q2):
        t0 = q2 * QT
        for st in range(NST):
            tk = st * T
            for kc in range(8):
                def mm(e, kc=kc, tk=tk):
                    return e.matmul(pX.ap()[0:8, :], rwsb.ap()[:, kc, :],
                                    hbB.ap()[:, kc, tk:tk + T], start=(kc == 0), stop=(kc == 7))
                w = ([f"dhb{q2}_{st}"] + bw("X")) if kc == 0 else []
                op("tensor", mm, waits=w, tag=f"trtB_{q2}_{st}" if kc == 7 else None)
            bank_w["X"] = f"trtB_{q2}_{st}"
            tg = [f"trtB_{q2}_{st}"]

            def chain(eng, fn, extra=(), tagn=None):
                t = tagn or f"rt{q2}_{st}_{chain.n}"
                chain.n += 1
                op(eng, fn, waits=([tg[0]] if tg[0] else []) + list(extra), tag=t, force=True)
                tg[0] = t
            chain.n = 0
            chain("vector", lambda e: e.tensor_scalar(rl8[:], pX.ap()[0:8, :], rb8col.ap(),
                                                      None, ALU.add), extra=["d_rb8"])
            bank_r["X"] = tg[0]
            for j in range(4):
                def tp(e, j=j):
                    return e.transpose(pX.ap()[:, j * 8:(j + 1) * 8],
                                       rl8.ap()[:, j * 128:(j + 1) * 128], id_sb[0:8, 0:8])
                chain("tensor", tp)
            bank_w["X"] = tg[0]
            chain("vector", lambda e: e.tensor_copy(rsc.ap()[:, 0:32], pX.ap()[:, 0:32]))
            bank_r["X"] = tg[0]
            v3 = lambda off: rsc.ap()[:, off:off + 32].rearrange("p (g x) -> p g x", g=4)
            chain("vector", lambda e: e.tensor_tensor(v3(96)[:, :, 0:4], v3(0)[:, :, 0:4],
                                                      v3(0)[:, :, 4:8], ALU.max))
            chain("vector", lambda e: e.tensor_tensor(v3(96)[:, :, 0:2], v3(96)[:, :, 0:2],
                                                      v3(96)[:, :, 2:4], ALU.max))
            chain("vector", lambda e: e.tensor_tensor(
                rsc.ap()[:, 160:164].rearrange("p (g x) -> p g x", x=1),
                v3(96)[:, :, 0:1], v3(96)[:, :, 1:2], ALU.max))
            for g in range(4):
                chain("vector", lambda e, g=g: e.tensor_scalar(
                    rsc.ap()[:, 32 + g * 8:40 + g * 8], rsc.ap()[:, g * 8:g * 8 + 8],
                    rsc.ap()[:, 160 + g:161 + g], None, ALU.subtract))
            chain("scalar", lambda e: e.activation(rsc.ap()[:, 32:64], rsc.ap()[:, 32:64],
                                                   AF.Exp))
            chain("vector", lambda e: e.tensor_tensor(v3(128)[:, :, 0:4], v3(32)[:, :, 0:4],
                                                      v3(32)[:, :, 4:8], ALU.add))
            chain("vector", lambda e: e.tensor_tensor(v3(128)[:, :, 0:2], v3(128)[:, :, 0:2],
                                                      v3(128)[:, :, 2:4], ALU.add))
            chain("vector", lambda e: e.tensor_tensor(
                rsc.ap()[:, 168:172].rearrange("p (g x) -> p g x", x=1),
                v3(128)[:, :, 0:1], v3(128)[:, :, 1:2], ALU.add))
            chain("scalar", lambda e: e.activation(rsc.ap()[:, 168:172], rsc.ap()[:, 168:172],
                                                   AF.Ln))
            chain("scalar", lambda e: e.activation(rsc.ap()[:, 168:172], rsc.ap()[:, 168:172],
                                                   AF.Exp, scale=-1.0))
            for g in range(4):
                chain("vector", lambda e, g=g: e.tensor_scalar(
                    rsc.ap()[:, 32 + g * 8:40 + g * 8], rsc.ap()[:, 32 + g * 8:40 + g * 8],
                    rsc.ap()[:, 168 + g:169 + g], None, ALU.mult))
            for g in range(4):
                chain("vector", lambda e, g=g: e.tensor_scalar(
                    rsc.ap()[:, 64 + g * 8:72 + g * 8], rsc.ap()[:, g * 8:g * 8 + 8],
                    rsc.ap()[:, 160 + g:161 + g], None, ALU.is_ge))
            chain("vector", lambda e: e.scalar_tensor_tensor(
                rsc.ap()[:, 96:128], rsc.ap()[:, 64:96], -1e30, rsc.ap()[:, 0:32],
                ALU.mult, ALU.add))
            chain("vector", lambda e: e.tensor_tensor(v3(128)[:, :, 0:4], v3(96)[:, :, 0:4],
                                                      v3(96)[:, :, 4:8], ALU.max))
            chain("vector", lambda e: e.tensor_tensor(v3(128)[:, :, 0:2], v3(128)[:, :, 0:2],
                                                      v3(128)[:, :, 2:4], ALU.max))
            chain("vector", lambda e: e.tensor_tensor(
                rsc.ap()[:, 164:168].rearrange("p (g x) -> p g x", x=1),
                v3(128)[:, :, 0:1], v3(128)[:, :, 1:2], ALU.max))
            for g in range(4):
                chain("vector", lambda e, g=g: e.tensor_scalar(
                    rsc.ap()[:, 128 + g * 8:136 + g * 8], rsc.ap()[:, 96 + g * 8:104 + g * 8],
                    rsc.ap()[:, 164 + g:165 + g], None, ALU.is_ge))
            chain("vector", lambda e: e.tensor_tensor(rsc.ap()[:, 64:96], rsc.ap()[:, 64:96],
                                                      rsc.ap()[:, 128:160], ALU.add))
            wbase = 192 + st * 32
            chain("vector", lambda e, wbase=wbase: e.tensor_tensor(
                rsc.ap()[:, wbase:wbase + 32], rsc.ap()[:, 32:64],
                rsc.ap()[:, 64:96], ALU.mult), tagn=f"vwt_{q2}_{st}")

            def part2(st=st, tk=tk, wbase=wbase):
                tgl = [f"vwt_{q2}_{st}"]
                for j in range(4):
                    def tpb(e, j=j):
                        return e.transpose(p[5].ap()[0:8, j * 128:(j + 1) * 128],
                                           rsc.ap()[:, wbase + j * 8:wbase + j * 8 + 8], id_sb[:])
                    w = tgl + (bw(5) if j == 0 else [])
                    tgl = []
                    op("tensor", tpb, waits=w, tag=f"twT_{q2}_{st}" if j == 3 else None)
                bank_w[5] = f"twT_{q2}_{st}"
                op("vector", lambda e, tk=tk, tok0=t0: e.tensor_copy(
                    w_hm.ap()[0:8, tok0 + tk:tok0 + tk + T], p[5].ap()[0:8, :]),
                    waits=[f"twT_{q2}_{st}"], tag=f"vw_{q2}_{st}", force=True)
                bank_r[5] = f"vw_{q2}_{st}"
            rt_p2[(q2, st)] = part2

    gP = 0
    dout_prev = {}
    for qq in range(NQ):
        tok0 = qq * QT
        if qq == 0:
            emit_dhb(0)
            emit_dacc(0)
            emit_router(0)

        def ln3_sq(st):
            tk = st * T
            laste = str(b_experts - 1) if b_experts > 0 else "s"
            for mc in range(8):
                op("scalar", lambda e, mc=mc, tk=tk: e.activation(
                    sq_bf.ap()[:, mc, :], acc.ap()[:, mc, tk:tk + T], AF.Square),
                    waits=[f"vac_{qq}_{laste}_{st}_7"] if mc == 0 else
                          [f"vac_{qq}_{laste}_{st}_{mc}"],
                    tag=f"sz3_{qq}_{st}_{mc}")

        def ln3_stats_p1(st, sq_done=False):
            tk = st * T
            laste = str(b_experts - 1) if b_experts > 0 else "s"
            if not sq_done:
                ln3_sq(st)
            for mc in range(8):
                def mm(e, mc=mc, tk=tk):
                    return e.matmul(pR.ap(), onescol_f[:], acc.ap()[:, mc, tk:tk + T],
                                    start=(mc == 0), stop=(mc == 7))
                w = [f"vac_{qq}_{laste}_{st}_{mc}"] + (bw("R") if mc == 0 else [])
                op("tensor", mm, waits=w, tag=f"tm3_{qq}_{st}" if mc == 7 else None)
            bank_w["R"] = f"tm3_{qq}_{st}"
            op("vector", lambda e: e.tensor_scalar(brow_nm[:], pR.ap(), -1.0 / D, None,
                                                   ALU.mult),
               waits=[f"tm3_{qq}_{st}"], tag=f"vnm3_{qq}_{st}")
            bank_r["R"] = f"vnm3_{qq}_{st}"
            for mc in range(8):
                def mm(e, mc=mc):
                    return e.matmul(pR.ap(), onescol_b[:], sq_bf.ap()[:, mc, :],
                                    start=(mc == 0), stop=(mc == 7))
                w = [f"sz3_{qq}_{st}_{mc}"] + (bw("R") if mc == 0 else [])
                op("tensor", mm, waits=w, tag=f"tsq3_{qq}_{st}" if mc == 7 else None)
            bank_w["R"] = f"tsq3_{qq}_{st}"
            op("vector", lambda e: e.tensor_scalar(brow_ms[:], pR.ap(), 1.0 / D, None,
                                                   ALU.mult),
               waits=[f"tsq3_{qq}_{st}"], tag=f"vms3_{qq}_{st}")
            bank_r["R"] = f"vms3_{qq}_{st}"
            op("vector", lambda e: e.tensor_tensor(brow_t[:], brow_nm[:], brow_nm[:],
                                                   ALU.mult))
            op("vector", lambda e: e.tensor_tensor(brow_t[:], brow_ms[:], brow_t[:],
                                                   ALU.subtract), tag=f"vv3_{qq}_{st}")
            op("scalar", lambda e: e.activation(brow_t[:], brow_t[:], AF.Ln,
                                                bias=cols.ap()[0:1, 197:198]),
               waits=[f"vv3_{qq}_{st}"])
            op("scalar", lambda e: e.activation(brow_ms[:], brow_t[:], AF.Exp, scale=-0.5),
               tag=f"ssd3_{qq}_{st}")
            op("vector", lambda e: e.tensor_tensor(brow_t[:], brow_nm[:], brow_ms[:],
                                                   ALU.mult),
               waits=[f"ssd3_{qq}_{st}"], tag=f"vr3_{qq}_{st}", force=True)
            for g in range(4):
                w = ([f"ssd3_{qq}_{st}"] + bw("X")) if g == 0 else []
                op("tensor", lambda e, g=g: e.transpose(
                    pX.ap()[:, g:g + 1], brow_ms.ap()[0:1, g * 128:(g + 1) * 128],
                    id_sb[0:1, 0:1]), waits=w)
            for g in range(4):
                w = [f"vr3_{qq}_{st}"] if g == 0 else []
                op("tensor", lambda e, g=g: e.transpose(
                    pX.ap()[:, 4 + g:5 + g], brow_t.ap()[0:1, g * 128:(g + 1) * 128],
                    id_sb[0:1, 0:1]), waits=w,
                    tag=f"trc_{qq}_{st}" if g == 3 else None)
            bank_w["X"] = f"trc_{qq}_{st}"
            wl3 = [f"trc_{qq}_{st}"]
            if qq > 0:
                wl3.append(f"s3_{qq-1}_{st}_3_1")
            op("vector", lambda e, st=st: e.tensor_copy(
                rsc.ap()[:, 172 + st * 8:180 + st * 8], pX.ap()[:, 0:8]),
               waits=wl3, tag=f"vrc_{qq}_{st}", force=True)
            bank_r["X"] = f"vrc_{qq}_{st}"
        def ln3_norm(st):
            tk = st * T
            for g in range(4):
                for half in range(2):
                    s = g * 2 + half
                    for j in range(4):
                        mc = half * 4 + j
                        pb3 = 4 + (s % 2)
                        def tp(e, mc=mc, g=g, tk=tk, j=j, pb3=pb3):
                            return e.transpose(p[pb3].ap()[:, j * 128:(j + 1) * 128],
                                               acc.ap()[:, mc, tk + g * 128:tk + (g + 1) * 128],
                                               id_sb[:])
                        w = bw(pb3) if j == 0 else []
                        op("tensor", tp, waits=w,
                           tag=f"tt3_{qq}_{st}_{g}_{half}" if j == 3 else None)
                    pb3 = 4 + (s % 2)
                    bank_w[pb3] = f"tt3_{qq}_{st}_{g}_{half}"
                    def ev3(e, g=g, half=half, pb3=pb3, st=st, s=s):
                        return e.activation(
                            out_tm.ap()[:, 0, (s % 2) * 512:(s % 2) * 512 + 512], p[pb3].ap(),
                            AF.Identity,
                            scale=rsc.ap()[:, 172 + st * 8 + g:173 + st * 8 + g],
                            bias=rsc.ap()[:, 176 + st * 8 + g:177 + st * 8 + g])
                    wl4 = [f"tt3_{qq}_{st}_{g}_{half}"]
                    if g == 0 and half == 0:
                        wl4.append(f"vrc_{qq}_{st}")
                    if dout_prev.get(s % 2):
                        wl4.append(dout_prev[s % 2])
                    op("scalar", ev3, waits=wl4, tag=f"s3_{qq}_{st}_{g}_{half}")
                    bank_r[pb3] = f"s3_{qq}_{st}_{g}_{half}"
                    tok = tok0 + tk + g * 128
                    op("sync", lambda e, tok=tok, half=half, s=s: e.dma_start(
                        out_hbm.ap()[tok:tok + 128, half * 512:(half + 1) * 512],
                        out_tm.ap()[:, 0, (s % 2) * 512:(s % 2) * 512 + 512]),
                        waits=[f"s3_{qq}_{st}_{g}_{half}"],
                        tag=f"dout_{qq}_{st}_{s}", dma=True, fam=f"dout{s % 2}")
                    dout_prev[s % 2] = f"dout_{qq}_{st}_{s}"
        pl = ([-1] + list(range(b_experts))) if b_experts > 0 else [-1]
        for pi, ei in enumerate(pl):
            ename = "s" if ei < 0 else str(ei)
            W1r = W1slots[gP % 2]
            first_pass = (pi == 0)
            for st in range(NST):
                hb = step % 2
                step += 1
                tk = st * T
                if pi == len(pl) - 1 and st == 1:
                    ln3_sq(0)
                if ei >= 0 and not first_pass:
                    wp = st
                    def mmw(e, wp=wp):
                        return e.matmul(p[4].ap(), ones1_b[:], wtmp[wp][0:1, :],
                                        start=True, stop=True)
                    op("tensor", mmw, waits=bw(4) + [f"vwsh_{qq}_{ei}_{st}"],
                       tag=f"twb_{qq}_{ei}_{st}")
                    bank_w[4] = f"twb_{qq}_{ei}_{st}"
                    h1_prev2[("wtmp", wp)] = f"twb_{qq}_{ei}_{st}"
                    op("scalar", lambda e: e.copy(wbc[:], p[4].ap()),
                       waits=[f"twb_{qq}_{ei}_{st}"], tag=f"swb_{qq}_{ei}_{st}")
                    bank_r[4] = f"swb_{qq}_{ei}_{st}"
                for mc in range(16):
                    b = pb_rot1[g1 % 2]
                    for kc in range(8):
                        def mm(e, b=b, kc=kc, mc=mc, tk=tk, W1r=W1r):
                            return e.matmul(p[b].ap(), W1r.ap()[:, kc, mc * 128:(mc + 1) * 128],
                                            hbB.ap()[:, kc, tk:tk + T],
                                            start=(kc == 0), stop=(kc == 7))
                        w = (bw(b) if kc == 0 else [])
                        if kc == 0 and mc == 0:
                            w = w + [f"dW1_{gP}", f"dhb{qq}_{st}"]
                        op("tensor", mm, waits=w,
                           tag=f"tmm1_{qq}_{ename}_{st}_{mc}" if kc == 7 else None)
                    bank_w[b] = f"tmm1_{qq}_{ename}_{st}_{mc}"
                    bias = (cols.ap()[:, C_SB1 + mc:C_SB1 + mc + 1] if ei < 0
                            else cols.ap()[:, C_RB1 + ei * 16 + mc:C_RB1 + ei * 16 + mc + 1])
                    wl2 = [f"tmm1_{qq}_{ename}_{st}_{mc}"]
                    if mc == 0 and hb in h1_prev2:
                        pv = h1_prev2[hb]
                        wl2 += pv if isinstance(pv, list) else [pv]
                    def ev(e, b=b, mc=mc, hb=hb, bias=bias):
                        return e.activation(h1[hb].ap()[:, mc, :], p[b].ap(), AF.Gelu, bias=bias)
                    op("scalar", ev, waits=wl2, tag=f"sg_{qq}_{ename}_{st}_{mc}")
                    bank_r[b] = f"sg_{qq}_{ename}_{st}_{mc}"
                    g1 += 1
                    if first_pass and mc == 6:
                        rt_p2[(qq, st)]()
                if ei >= 0:
                    for mc in range(16):
                        wl2 = [f"sg_{qq}_{ename}_{st}_{mc}"] +                             ([f"swb_{qq}_{ei}_{st}"] if mc == 0 else [])
                        op("vector", lambda e, mc=mc, hb=hb: e.tensor_tensor(
                            h1[hb].ap()[:, mc, :], h1[hb].ap()[:, mc, :], wbc[:], ALU.mult),
                            waits=wl2, tag=f"vsc_{qq}_{ei}_{st}" if mc == 15 else None)
                if pi == len(pl) - 1 and st == 1:
                    ln3_stats_p1(0, sq_done=True)
                nkc = 16
                for mc in range(8):
                    b = pb_rot2[g2 % 2]
                    for kc in range(nkc):
                        def mm2(e, b=b, kc=kc, mc=mc, hb=hb, ei=ei):
                            last = (kc == nkc - 1) and (ei != 0)
                            return e.matmul(p[b].ap(), W2.ap()[:, kc, mc * 128:(mc + 1) * 128],
                                            h1[hb].ap()[:, kc, :],
                                            start=(kc == 0), stop=last)
                        w = (bw(b) if kc == 0 else [])
                        if kc == 0 and mc == 0:
                            w = w + [f"dW2_{gP}"]
                            w = w + ([f"vsc_{qq}_{ei}_{st}"] if ei >= 0
                                     else [f"sg_{qq}_{ename}_{st}_15"])
                        op("tensor", mm2, waits=w,
                           tag=(f"tmm2_{qq}_{ename}_{st}_{mc}" if (kc == nkc - 1 and ei != 0) else None))
                    if ei == 0:
                        def mmr(e, b=b, mc=mc, tok0=tok0, tk=tk):
                            return e.matmul(p[b].ap(), rb2_sb.ap()[:, mc * 128:(mc + 1) * 128],
                                            w_hm.ap()[0:8, tok0 + tk:tok0 + tk + T],
                                            start=False, stop=True)
                        op("tensor", mmr, waits=([f"vw_{qq}_{st}"] if mc == 0 else []),
                           tag=f"tmm2_{qq}_{ename}_{st}_{mc}")
                    bank_w[b] = f"tmm2_{qq}_{ename}_{st}_{mc}"
                    wl2 = [f"tmm2_{qq}_{ename}_{st}_{mc}"] + \
                        ([f"dacc{qq}_{mc}"] if first_pass else [])
                    if ei < 0:
                        def ev2(e, b=b, mc=mc, tk=tk):
                            return e.scalar_tensor_tensor(
                                acc.ap()[:, mc, tk:tk + T], p[b].ap(),
                                cols.ap()[:, C_SB2 + mc:C_SB2 + mc + 1],
                                acc.ap()[:, mc, tk:tk + T], ALU.add, ALU.add)
                    else:
                        def ev2(e, b=b, mc=mc, tk=tk):
                            return e.tensor_tensor(acc.ap()[:, mc, tk:tk + T],
                                                   acc.ap()[:, mc, tk:tk + T], p[b].ap(), ALU.add)
                    op("vector", ev2, waits=wl2, tag=f"vac_{qq}_{ename}_{st}_{mc}")
                    bank_r[b] = f"vac_{qq}_{ename}_{st}_{mc}"
                    g2 += 1
                h1_prev2[hb] = f"tmm2_{qq}_{ename}_{st}_7"
                if pi == len(pl) - 1 and st == 1:
                    if qq + 1 < NQ:
                        emit_dhb(qq + 1)
                    ln3_stats_p1(1)
                    if qq + 1 < NQ:
                        emit_router(qq + 1)
            if gP + 2 < nP and passes[gP + 2][0] <= qq + 1:
                op("gpsimd", lambda e, g=gP: e.dma_start(
                    W1slots[g % 2].ap(), w1_src(passes[g + 2][1])),
                    waits=[f"tmm1_{qq}_{ename}_1_15"], tag=f"dW1_{gP+2}", dma=True, fam="dw1")
            if gP + 1 < nP:
                op("gpsimd", lambda e, g=gP: e.dma_start(W2.ap(), w2_src(passes[g + 1][1])),
                    waits=[f"tmm2_{qq}_{ename}_1_7"], tag=f"dW2_{gP+1}", dma=True, fam="dw2")
            if pi + 1 < len(pl) and pl[pi + 1] >= 0:
                nei = pl[pi + 1]
                for st2 in range(NST):
                    wp = st2
                    op("vector", lambda e, nei=nei, st2=st2, wp=wp, tok0=tok0: e.stream_shuffle(
                        wtmp[wp][:], w_hm.ap()[:, tok0 + st2 * T:tok0 + st2 * T + T],
                        [nei] * 32),
                        waits=[f"vw_{qq}_{st2}"] +
                              ([h1_prev2[("wtmp", wp)]] if ("wtmp", wp) in h1_prev2 else []),
                        tag=f"vwsh_{qq}_{nei}_{st2}")
            gP += 1
        for st in range(NST):
            ln3_norm(st)
        if qq + 1 < NQ:
            emit_dacc(qq + 1)
        h1_prev2[0] = f"tsq3_{qq}_{NST-1}"
        h1_prev2[1] = [f"dout_{qq}_{NST-1}_6", f"dout_{qq}_{NST-1}_7"]
        if qq == NQ - 1:
            dump("w", lambda: w_hm.ap(), [f"vw_{qq}_{NST-1}"])
    op("sync", None, waits=[f"dout_{NQ-1}_{NST-1}_6", f"dout_{NQ-1}_{NST-1}_7"])
    emit_all(nc, prog)
    return nc


_HMAT = np.zeros((128, 4, NH), dtype=ml_dtypes.bfloat16)
for _c in range(4):
    for _r in range(128):
        _HMAT[_r, _c, 2 * _c + _r // 64] = 0.125
_MCONST = np.zeros((NH, INNER), dtype=np.float32)
for _h in range(NH):
    _MCONST[_h, _h * 64:(_h + 1) * 64] = 1.0
_ID128 = np.eye(128, dtype=np.float32)


def kernel(**inputs):
    nc = build_nc()
    common = {
        "Wq": inputs["L0_attn_Wq"], "Wk": inputs["L0_attn_Wk"], "Wv": inputs["L0_attn_Wv"],
        "Wo": inputs["L0_attn_Wo"], "bq": inputs["L0_attn_bq"], "bk": inputs["L0_attn_bk"],
        "bv": inputs["L0_attn_bv"], "bo": inputs["L0_attn_bo"],
        "alpha": np.ascontiguousarray(inputs["L0_attn_alpha"].reshape(-1)),
        "augp": np.ascontiguousarray(inputs["L0_attn_aug_param"].reshape(-1)),
        "lns": inputs["L0_attn_ln_scale"], "lnb": inputs["L0_attn_ln_bias"],
        "rw": inputs["L0_ffn_router_w"], "rb": inputs["L0_ffn_router_b"],
        "sw1": inputs["L0_ffn_sw1"], "sb1": inputs["L0_ffn_sb1"],
        "sw2": inputs["L0_ffn_sw2"], "sb2": inputs["L0_ffn_sb2"],
        "rw1": inputs["L0_ffn_rw1"], "rb1": inputs["L0_ffn_rb1"],
        "rw2": inputs["L0_ffn_rw2"], "rb2": inputs["L0_ffn_rb2"],
        "id128": _ID128, "hmat": _HMAT, "mconst": _MCONST,
    }
    common = {k: np.ascontiguousarray(np.asarray(v)) for k, v in common.items()}
    x = np.asarray(inputs["L0_attn_x"])
    in_maps = [dict(common, x=np.ascontiguousarray(x[c])) for c in range(NCORES)]
    res = run_bass_kernel_spmd(nc, in_maps, core_ids=list(range(NCORES)))
    return np.stack([res.results[c]["out"] for c in range(NCORES)], axis=0)
